# version 10
# baseline (speedup 1.0000x reference)
import numpy as np
import ml_dtypes
from contextlib import ExitStack

import concourse.bass as bass
import concourse.mybir as mybir
from concourse.bass_utils import run_bass_kernel_spmd
from concourse.bass_core_rust import IndirectOffsetOnAxis
from concourse.aot_env import allow_ici_all_to_all

F32 = mybir.dt.float32
BF16 = mybir.dt.bfloat16
U32 = mybir.dt.uint32
AF = mybir.ActivationFunctionType
ALU = mybir.AluOpType
AX = mybir.AxisListType.X

B, L, D = 2, 8192, 1024
T = 2048
DI, DS, II, E = 2048, 16, 2816, 7
ND = D // 128
NF = 4096 // 128
NI = DI // 128
NII = II // 128
NTT = T // 128
TS = 512
NTS = T // TS
CH = DI // 4
NW = 4
CC = 512
FWD_ROWS = 1056
LN_EPS = 1e-5
NGC = 100
BS = 2048

ARB_N = 67584
ARF_N = 14336
AB = {
    "xc": 0, "aT": 32768, "wbfA": 49152, "wbfB": 51200,
    "xpcA": 53248, "xpcB": 55304, "evbA": 57360, "evbB": 59408,
    "bcb": 61456, "caccb": 63504,
    "dtw": 0, "uw": 8192, "dtuw": 16384, "h2": 24576,
    "dax": 28672, "btx": 32768, "bc2": 36864,
    "hc2": 53248, "carry": 57344,
    "hT": 0, "gated": 16384, "outcm": 49152, "ytile": 53248,
    "rstile": 55296, "silur": 57344, "wbfC": 59392, "wbfD": 61440,
    "gg": 16384, "w2h": 38912, "acc": 50176, "wbc": 66560,
    "w13A": 38912, "w13B": 43008,
}
AFm = {
    "xtA": 0, "xtB": 1024, "xh": 2048, "scr1": 3072, "x0A": 4096,
    "x0B": 5120, "x1t": 6144, "zt": 7168, "mt": 8192, "at2": 9216,
    "wstA": 10240, "wstB": 12288,
}


def _host_prep(inputs):
    f32 = np.float32
    asc = np.ascontiguousarray
    x = np.asarray(inputs["L0_attn_x"], f32)
    win = np.asarray(inputs["L0_attn_in_proj_w"], f32)
    convw = np.asarray(inputs["L0_attn_conv_w"], f32)[:, 0, :]
    convb = np.asarray(inputs["L0_attn_conv_b"], f32)
    wx = np.asarray(inputs["L0_attn_x_proj_w"], f32)
    wdt = np.asarray(inputs["L0_attn_dt_proj_w"], f32)
    dtb = np.asarray(inputs["L0_attn_dt_proj_b"], f32)
    alog = np.asarray(inputs["L0_attn_A_log"], f32)
    dvec = np.asarray(inputs["L0_attn_D"], f32)
    wout = np.asarray(inputs["L0_attn_out_proj_w"], f32)
    gamma = np.asarray(inputs["L0_attn_ln_gamma"], f32)
    beta = np.asarray(inputs["L0_attn_ln_beta"], f32)
    rw = np.asarray(inputs["L0_ffn_router_w"], f32)
    rb = np.asarray(inputs["L0_ffn_routing_bias"], f32)
    sw1 = np.asarray(inputs["L0_ffn_sw1"], f32)
    sw2 = np.asarray(inputs["L0_ffn_sw2"], f32)
    sw3 = np.asarray(inputs["L0_ffn_sw3"], f32)
    ew1 = np.asarray(inputs["L0_ffn_ew1"], f32)
    ew2 = np.asarray(inputs["L0_ffn_ew2"], f32)
    ew3 = np.asarray(inputs["L0_ffn_ew3"], f32)

    oh = np.zeros((32, 32, 128), dtype=ml_dtypes.bfloat16)
    for j in range(32):
        oh[:, j, :][j, :] = 1.0
    shared = {
        "winT": asc(win.T),
        "wxT": asc(wx.T),
        "wdtT": asc(wdt.T),
        "woutT": asc(wout.T),
        "convw_r": asc(convw.reshape(NI, 128, 4).transpose(1, 0, 2).reshape(128, NI * 4)),
        "convb_r": asc(convb.reshape(NI, 128).T),
        "dtb_r": asc(dtb.reshape(NI, 128).T),
        "alog_r": asc(np.tile(alog, (128, 1))),
        "gamma_r": asc(gamma.reshape(1, D)),
        "beta_r": asc(beta.reshape(1, D)),
        "rwT": asc(rw.T),
        "rbias_r": asc(np.tile(rb, (128, 1))),
        "w1T": asc(np.concatenate([ew1.transpose(0, 2, 1), sw1.T[None]], 0)),
        "w3T": asc(np.concatenate([ew3.transpose(0, 2, 1), sw3.T[None]], 0)),
        "w2T": asc(np.concatenate([ew2.transpose(0, 2, 1), sw2.T[None]], 0)),
        "ident_b": np.eye(128, dtype=ml_dtypes.bfloat16),
        "ones_b": np.ones((1, 128), dtype=ml_dtypes.bfloat16),
        "ones_f": np.ones((1, 128), dtype=f32),
        "onehot": asc(oh.reshape(32, 32 * 128)),
    }

    w1b = shared["w1T"].reshape(8, 1024, 11, 256)
    w3b = shared["w3T"].reshape(8, 1024, 11, 256)
    shared["w13R"] = asc(np.concatenate([w1b, w3b], axis=-1).transpose(0, 2, 1, 3))
    del shared["w1T"], shared["w3T"]

    per_core = []
    for c in range(8):
        b, g = c // 4, c % 4
        xs = asc(x[b, g * T:(g + 1) * T])
        halo = np.zeros((3, D), f32) if g == 0 else asc(x[b, g * T - 3:g * T])
        dvec_my = asc(dvec.reshape(NI, 128).T[:, g * 4:(g + 1) * 4])
        gidx = np.zeros((128, NGC), np.uint32)
        base = 0 if c < 4 else 4 * FWD_ROWS
        ybase = 0 if c < 4 else 4 * CH
        r = np.arange(128, dtype=np.uint32)
        for w in range(NW):
            for ib in range(4):
                gidx[:, w * 9 + ib] = base + w * FWD_ROWS + ib * 128 + r
                gidx[:, w * 9 + 4 + ib] = base + w * FWD_ROWS + 512 + ib * 128 + r
            gidx[0:32, w * 9 + 8] = base + w * FWD_ROWS + 1024 + np.arange(32, dtype=np.uint32)
        for gs in range(4):
            for isub in range(4):
                gidx[:, 36 + gs * 4 + isub] = ybase + gs * CH + isub * 128 + r
        for w in range(NW):
            for sp2 in range(8):
                gidx[:, 52 + w * 8 + sp2] = base + w * FWD_ROWS + 1024 + 4 * sp2
        for e_i in range(8):
            for tsp in range(2):
                gidx[:, 84 + e_i * 2 + tsp] = e_i * 16 + tsp * 8
        per_core.append({"x": xs, "xhalo": halo, "dvec_my": dvec_my, "gidx": gidx})
    return shared, per_core


class Sched:
    ENG_SEM = {"tensor": "tv", "vector": "vv", "scalar": "sv",
               "gpsimd": "pv", "sync": "yv"}

    def __init__(self):
        self.ops = []
        self.cnt = {}
        self.last_w = {}
        self.readers = {}

    def _tok(self, sem, amt):
        self.cnt[sem] = self.cnt.get(sem, 0) + amt
        return (sem, self.cnt[sem])

    def op(self, eng, fn, reads=(), writes=(), sem=None, amt=1, deps=()):
        waits = {}

        def need(tok):
            if tok is None:
                return
            s, v = tok
            if waits.get(s, 0) < v:
                waits[s] = v
        for bn in reads:
            need(self.last_w.get(bn))
        for bn in writes:
            need(self.last_w.get(bn))
            for t in self.readers.get(bn, ()):
                need(t)
        for t in deps:
            need(t)
        if sem is None:
            sem = self.ENG_SEM[eng]
        tok = self._tok(sem, amt)
        self.ops.append((eng, waits, fn, sem, amt))
        for bn in writes:
            self.last_w[bn] = tok
            self.readers[bn] = []
        for bn in reads:
            self.readers.setdefault(bn, []).append(tok)
        return tok

    def dma(self, queue, fn, reads=(), writes=(), deps=()):
        qsem = {"sync": "ld", "gpsimd": "gq", "scalar": "aq"}[queue]
        return self.op(queue, fn, reads=reads, writes=writes, sem=qsem,
                       amt=16, deps=deps)

    def emit_engine(self, eng_name, eng, sems):
        wm = {}
        for en, waits, fn, sem, amt in self.ops:
            if en != eng_name:
                continue
            for s, v in sorted(waits.items()):
                if wm.get(s, 0) < v:
                    eng.wait_ge(sems[s], v)
                    wm[s] = v
            inst = fn(eng)
            assert inst is not None
            inst.then_inc(sems[sem], amt)


def bkeys(name, off, n):
    c0 = AB[name] + off
    return ["ab%d" % i for i in range(c0 // BS, -(-(c0 + n) // BS))]


def fkeys(name, off, n):
    c0 = AFm[name] + off
    return ["af%d" % i for i in range(c0 // BS, -(-(c0 + n) // BS))]


def build_program(stages=("p1", "p2", "p3", "p4", "p5"), debug=()):
    nc = bass.Bass("TRN2", target_bir_lowering=False, debug=False)
    S = Sched()

    def dram_in(name, shape, dt=F32):
        return nc.dram_tensor(name, list(shape), dt, kind="ExternalInput").ap()

    X = dram_in("x", (T, D))
    XH = dram_in("xhalo", (3, D))
    WINT = dram_in("winT", (D, 4096))
    WXT = dram_in("wxT", (DI, 32))
    WDTT = dram_in("wdtT", (DI, DI))
    WOUTT = dram_in("woutT", (DI, D))
    CONVW = dram_in("convw_r", (128, NI * 4))
    CONVB = dram_in("convb_r", (128, NI))
    DTB = dram_in("dtb_r", (128, NI))
    ALOG = dram_in("alog_r", (128, NI))
    DVECM = dram_in("dvec_my", (128, 4))
    GAMMA = dram_in("gamma_r", (1, D))
    BETA = dram_in("beta_r", (1, D))
    RWT = dram_in("rwT", (D, 7))
    RBIAS = dram_in("rbias_r", (128, 7))
    W13R = dram_in("w13R", (E + 1, 11, D, 512))
    W2T = dram_in("w2T", (E + 1, II, D))
    IDENT = dram_in("ident_b", (128, 128), BF16)
    ONESB = dram_in("ones_b", (1, 128), BF16)
    ONESF = dram_in("ones_f", (1, 128))
    OH = dram_in("onehot", (32, 32 * 128), BF16)
    GIDX = dram_in("gidx", (128, NGC), U32)

    OUT = nc.dram_tensor("out", [T, D], F32, kind="ExternalOutput").ap()

    contrib_fwd = nc.dram_tensor("contrib_fwd", [8 * FWD_ROWS, T], BF16).ap()
    a2a_fwd = nc.dram_tensor("a2a_fwd", [8 * FWD_ROWS, T], BF16).ap()
    contrib_y = nc.dram_tensor("contrib_y", [8 * CH, T], BF16).ap()
    a2a_y = nc.dram_tensor("a2a_y", [8 * CH, T], BF16).ap()
    res_d = nc.dram_tensor("res_d", [DI, T], BF16).ap()
    x1_d = nc.dram_tensor("x1_d", [T, D], F32).ap()
    w8_d = nc.dram_tensor("w8_d", [128, 128], BF16).ap()
    w8r_d = nc.dram_tensor("w8r_d", [128, 128], BF16).ap()

    dbg_outs = {}

    def dbg(name, shape, dt=F32):
        ap = nc.dram_tensor("dbg_" + name, list(shape), dt, kind="ExternalOutput").ap()
        dbg_outs[name] = ap
        return ap

    with ExitStack() as ctx:
        def sb(name, shape, dt=F32):
            return ctx.enter_context(nc.sbuf_tensor(name, list(shape), dt))

        def psum(name, shape, dt=F32):
            return ctx.enter_context(nc.psum_tensor(name, list(shape), dt))

        ARB = sb("ARB", (128, ARB_N), BF16)
        ARF = sb("ARF", (128, ARF_N))

        def ab(name, off=0, n=None, r0=0, r1=128):
            c0 = AB[name] + off
            return ARB[r0:r1, c0:c0 + n]

        def af(name, off=0, n=None, r0=0, r1=128):
            c0 = AFm[name] + off
            return ARF[r0:r1, c0:c0 + n]

        gidx_s = sb("gidx_s", (128, NGC), U32)
        alog_s = sb("alog_s", (128, NI))
        convw_s = sb("convw_s", (128, NI * 4))
        convb_s = sb("convb_s", (128, NI))
        dtb_s = sb("dtb_s", (128, NI))
        dvec_s = sb("dvec_s", (128, 4))
        rbias_s = sb("rbias_s", (128, 7))
        oh_s = sb("oh_s", (32, 32 * 128), BF16)
        id_s = sb("id_s", (128, 128), BF16)
        onesb_s = sb("onesb_s", (1, 128), BF16)
        onesf_s = sb("onesf_s", (1, 128))
        gb_s = sb("gb_s", (128, 2 * D))
        ones7 = sb("ones7", (128, 8))
        zero7 = sb("zero7", (128, 8))
        rwT_s = sb("rwT_s", (128, ND * 7), BF16)
        aTh = sb("aTh", (128, 24), BF16)
        rwstage = sb("rwstage", (128, ND * 7))
        stat = sb("stat", (128, 16))
        lg_t = sb("lg_t", (128, 8))
        prb = sb("prb", (128, 8))
        pmax = sb("pmax", (128, 1))
        pm7 = sb("pm7", (128, 8))
        equ = sb("equ", (128, 8), U32)
        eff = sb("eff", (128, 8))
        pre = sb("pre", (128, 8))
        msk = sb("msk", (128, 8))
        pclp = sb("pclp", (128, 4))
        w8cat = sb("w8cat", (128, 128))
        w8b = sb("w8b", (128, 128), BF16)

        def wrow_ap(c0, n):
            return ARB[0:1, AB["wbc"] + c0: AB["wbc"] + c0 + n]

        pbank = [psum("pb%d" % i, (128, 512)) for i in range(7)]
        pt = psum("pt", (128, 1024), BF16)
        PB = ["pb%d" % i for i in range(7)]

        sem_names = ["ld", "gq", "aq", "tv", "vv", "sv", "pv", "yv", "cfw", "cy"]
        sems = {n: ctx.enter_context(nc.semaphore(n)) for n in sem_names}
        block = ctx.enter_context(nc.Block())

        for ap_, t_, nm in [(GIDX, gidx_s, "gidx"), (ALOG, alog_s, "alog"),
                            (CONVW, convw_s, "convw"), (CONVB, convb_s, "convb"),
                            (DTB, dtb_s, "dtb"), (DVECM, dvec_s, "dvec"),
                            (RBIAS, rbias_s, "rbias"), (OH, oh_s, "oh"),
                            (IDENT, id_s, "id"), (ONESB, onesb_s, "onesb"),
                            (ONESF, onesf_s, "onesf")]:
            S.dma("sync", lambda e, ap_=ap_, t_=t_: e.dma_start(out=t_[:, :], in_=ap_),
                  writes=[nm])
        S.op("vector", lambda e: e.memset(ones7[:, :], 1.0), writes=["ones7"])
        S.op("vector", lambda e: e.memset(zero7[:, :], 0.0), writes=["zero7"])
        S.dma("sync", lambda e: e.dma_start(out=af("wstA", 0, D, 0, 1), in_=GAMMA),
              writes=fkeys("wstA", 0, 2 * D))
        S.dma("sync", lambda e: e.dma_start(out=af("wstA", D, D, 0, 1), in_=BETA),
              writes=fkeys("wstA", 0, 2 * D))
        for j in range(4):
            S.op("tensor",
                 lambda e, j=j: e.matmul(pbank[4][:, :], onesf_s[0:1, :],
                                         af("wstA", j * 512, 512, 0, 1),
                                         start=True, stop=True),
                 reads=fkeys("wstA", 0, 2 * D) + ["onesf"], writes=["pb4"])
            S.op("vector",
                 lambda e, j=j: e.tensor_copy(gb_s[:, j * 512:(j + 1) * 512],
                                              pbank[4][:, :]),
                 reads=["pb4"], writes=["gb"])
        S.dma("sync", lambda e: e.dma_start(
            out=rwstage[:, :].rearrange("p (c m) -> p c m", m=7),
            in_=RWT.rearrange("(c p) m -> p c m", p=128)),
            writes=["rwst"])
        S.op("vector", lambda e: e.tensor_copy(rwT_s[:, :], rwstage[:, :]),
             reads=["rwst"], writes=["rwT"])

        def emit_ln(src_ap, src_keys, dst_ap, dst_keys, par=0):
            o = par * 8
            k = ["st%d_%d" % (i, par) for i in range(8)]
            S.op("vector", lambda e: e.reduce_sum(stat[:, o:o + 1], src_ap, axis=AX),
                 reads=src_keys, writes=[k[0]])
            S.op("vector", lambda e: e.tensor_scalar(stat[:, o + 1:o + 2],
                                                     stat[:, o:o + 1],
                                                     1.0 / D, None, ALU.mult),
                 reads=[k[0]], writes=[k[1]])
            S.op("scalar", lambda e: e.activation(dst_ap, src_ap, AF.Square,
                                                  accum_out=stat[:, o + 2:o + 3]),
                 reads=src_keys, writes=dst_keys + [k[2]])
            S.op("vector", lambda e: e.tensor_scalar(stat[:, o + 3:o + 4],
                                                     stat[:, o + 2:o + 3],
                                                     1.0 / D, LN_EPS, ALU.mult,
                                                     ALU.add),
                 reads=[k[2]], writes=[k[3]])
            S.op("vector", lambda e: e.tensor_tensor(stat[:, o + 4:o + 5],
                                                     stat[:, o + 1:o + 2],
                                                     stat[:, o + 1:o + 2], ALU.mult),
                 reads=[k[1]], writes=[k[4]])
            S.op("vector", lambda e: e.tensor_tensor(stat[:, o + 5:o + 6],
                                                     stat[:, o + 3:o + 4],
                                                     stat[:, o + 4:o + 5], ALU.subtract),
                 reads=[k[3], k[4]], writes=[k[5]])
            S.op("scalar", lambda e: e.activation(stat[:, o + 6:o + 7],
                                                  stat[:, o + 5:o + 6], AF.Sqrt),
                 reads=[k[5]], writes=[k[6]])
            S.op("vector", lambda e: e.reciprocal(stat[:, o + 7:o + 8],
                                                  stat[:, o + 6:o + 7]),
                 reads=[k[6]], writes=[k[7]])
            return S.op("vector", lambda e: e.tensor_scalar(
                dst_ap, src_ap, stat[:, o + 1:o + 2], stat[:, o + 7:o + 8],
                ALU.subtract, ALU.mult),
                reads=src_keys + [k[1], k[7]], writes=dst_keys)

        if "p1" in stages:
            S.op("vector", lambda e: e.memset(af("xh", 0, D), 0.0), writes=fkeys("xh", 0, D))
            S.dma("sync", lambda e: e.dma_start(out=af("xh", 0, D, 0, 3), in_=XH),
                  writes=fkeys("xh", 0, D))
            emit_ln(af("xh", 0, D), fkeys("xh", 0, D),
                    ab("evbA", 0, D), bkeys("evbA", 0, D))
            def tf_halo(e):
                last = None
                for j in range(ND):
                    last = e.transpose(pt[:, j * 128:(j + 1) * 128],
                                       ab("evbA", j * 128, 128), id_s[:, :])
                return last
            S.op("tensor", tf_halo, reads=bkeys("evbA", 0, D) + ["id"], writes=["pt"])
            def ev_halo(e):
                return e.tensor_copy(
                    aTh[:, :].rearrange("p (c m) -> p c m", m=3),
                    pt[:, 0:1024].rearrange("p (c m) -> p c m", m=128)[:, :, 0:3])
            S.op("vector", ev_halo, reads=["pt"], writes=["aTh"])

            for tt in range(NTT):
                par = "AB"[tt % 2]
                xa = af("xt" + par, 0, D)
                xk = fkeys("xt" + par, 0, D)
                S.dma("sync", lambda e, tt=tt, xa=xa: e.dma_start(
                    out=xa, in_=X[tt * 128:(tt + 1) * 128, :]), writes=xk)
                emit_ln(xa, xk, ab("evb" + par, 0, D), bkeys("evb" + par, 0, D),
                        par=tt % 2)
                def tf(e, par=par):
                    last = None
                    for j in range(ND):
                        last = e.transpose(pt[:, j * 128:(j + 1) * 128],
                                           ab("evb" + par, j * 128, 128), id_s[:, :])
                    return last
                S.op("tensor", tf, reads=bkeys("evb" + par, 0, D) + ["id"], writes=["pt"])
                def ev(e, tt=tt):
                    dst = ARB[:, AB["aT"]: AB["aT"] + ND * T] \
                        .rearrange("p (c m) -> p c m", m=T)[:, :, tt * 128:(tt + 1) * 128]
                    src = pt[:, 0:1024].rearrange("p (c m) -> p c m", m=128)
                    return e.tensor_copy(dst, src)
                S.op("vector", ev, reads=["pt"],
                     writes=bkeys("aT", 0, ND * T))



        if "p1" in stages:
            for f in range(NF):
                par = "AB"[f % 2]
                wstk = fkeys("wst" + par, 0, DI)
                wbfk = bkeys("wbf" + par, 0, DI)
                S.dma("sync", lambda e, f=f, par=par: e.dma_start(
                    out=af("wst" + par, 0, 1024).rearrange("p (c m) -> p c m", m=128),
                    in_=WINT[:, f * 128:(f + 1) * 128]
                        .rearrange("(c p) m -> p c m", p=128)),
                    writes=wstk)
                S.op("scalar", lambda e, par=par: e.copy(
                    ab("wbf" + par, 0, 1024), af("wst" + par, 0, 1024)),
                    reads=wstk, writes=wbfk)

                def mm(e, f=f, par=par):
                    last = None
                    for k in range(ND):
                        lhs = ab("wbf" + par, k * 128, 128)
                        for ts in range(NTS):
                            rhs = ARB[:, AB["aT"] + k * T + ts * TS:
                                      AB["aT"] + k * T + (ts + 1) * TS]
                            last = e.matmul(pbank[ts][:, :], lhs, rhs,
                                            start=(k == 0), stop=(k == ND - 1))
                    return last
                S.op("tensor", mm, reads=wbfk + bkeys("aT", 0, ND * T),
                     writes=["pb0", "pb1", "pb2", "pb3"])

                if f < NI:
                    xpk = bkeys("xpc" + par, 0, 3 + T)
                    for ts in range(NTS):
                        S.op("vector", lambda e, ts=ts, par=par: e.tensor_copy(
                            ab("xpc" + par, 3 + ts * TS, TS), pbank[ts][:, :]),
                            reads=[PB[ts]], writes=xpk)
                    def mmh(e, f=f, par=par):
                        last = None
                        for k in range(ND):
                            last = e.matmul(
                                pbank[4][:, 0:3], ab("wbf" + par, k * 128, 128),
                                aTh[:, k * 3:(k + 1) * 3],
                                start=(k == 0), stop=(k == ND - 1))
                        return last
                    S.op("tensor", mmh,
                         reads=bkeys("wbf" + par, 0, DI) + ["aTh"],
                         writes=["pb4"])
                    S.op("vector", lambda e, par=par: e.tensor_copy(
                        ab("xpc" + par, 0, 3), pbank[4][:, 0:3]),
                        reads=["pb4"], writes=xpk)
                    S.op("vector", lambda e, f=f, par=par: e.tensor_scalar(
                        ab("caccb", 0, T), ab("xpc" + par, 0, T),
                        convw_s[:, f * 4:f * 4 + 1], None, ALU.mult),
                        reads=xpk + ["convw"], writes=bkeys("caccb", 0, T))
                    for k in range(1, 4):
                        S.op("vector", lambda e, f=f, k=k, par=par:
                             e.scalar_tensor_tensor(
                                 ab("caccb", 0, T), ab("xpc" + par, k, T),
                                 convw_s[:, f * 4 + k:f * 4 + k + 1],
                                 ab("caccb", 0, T), ALU.mult, ALU.add),
                             reads=xpk + ["convw"] + bkeys("caccb", 0, T),
                             writes=bkeys("caccb", 0, T))
                    S.op("scalar", lambda e, f=f: e.activation(
                        ab("xc", f * T, T), ab("caccb", 0, T), AF.Silu,
                        bias=convb_s[:, f:f + 1]),
                        reads=bkeys("caccb", 0, T) + ["convb"],
                        writes=bkeys("xc", f * T, T))
                    g, rr = f // 4, (f % 4) * 128
                    for half in (0, 4):
                        S.dma("gpsimd", lambda e, g=g, rr=rr, half=half, f=f:
                              e.dma_start(
                                  out=contrib_fwd[(g + half) * FWD_ROWS + 512 + rr:
                                                  (g + half) * FWD_ROWS + 512 + rr + 128, :],
                                  in_=ab("xc", f * T, T)),
                              reads=bkeys("xc", f * T, T),
                              writes=["cfwd_u%d_%d" % (f, half)])
                else:
                    fr = f - NI
                    for ts in range(NTS):
                        S.op("vector", lambda e, ts=ts, par=par: e.tensor_copy(
                            ab("evb" + par, ts * TS, TS), pbank[ts][:, :]),
                            reads=[PB[ts]], writes=bkeys("evb" + par, 0, T))
                    S.dma("gpsimd", lambda e, fr=fr, par=par: e.dma_start(
                        out=res_d[fr * 128:(fr + 1) * 128, :], in_=ab("evb" + par, 0, T)),
                        reads=bkeys("evb" + par, 0, T), writes=["resd%d" % fr])

        if "p1" in stages:
            for f in range(NI):
                par = "AB"[f % 2]
                wstk = fkeys("wst" + par, 0, DI)
                wbfk = bkeys("wbf" + par, 0, DI)
                S.dma("sync", lambda e, f=f, par=par: e.dma_start(
                    out=af("wst" + par, 0, DI).rearrange("p (c m) -> p c m", m=128),
                    in_=WDTT[:, f * 128:(f + 1) * 128]
                        .rearrange("(c p) m -> p c m", p=128)),
                    writes=wstk)
                S.op("scalar", lambda e, par=par: e.copy(
                    ab("wbf" + par, 0, DI), af("wst" + par, 0, DI)),
                    reads=wstk, writes=wbfk)

                def mm(e, par=par):
                    last = None
                    for k in range(NI):
                        lhs = ab("wbf" + par, k * 128, 128)
                        for ts in range(NTS):
                            rhs = ARB[:, AB["xc"] + k * T + ts * TS:
                                      AB["xc"] + k * T + (ts + 1) * TS]
                            last = e.matmul(pbank[ts][:, :], lhs, rhs,
                                            start=(k == 0), stop=(k == NI - 1))
                    return last
                S.op("tensor", mm, reads=wbfk + bkeys("xc", 0, NI * T),
                     writes=["pb0", "pb1", "pb2", "pb3"])
                for ts in range(NTS):
                    S.op("vector", lambda e, ts=ts, f=f, par=par: e.tensor_scalar(
                        ab("evb" + par, ts * TS, TS), pbank[ts][:, :],
                        dtb_s[:, f:f + 1], None, ALU.add),
                        reads=[PB[ts], "dtb"], writes=bkeys("evb" + par, 0, T))
                g, rr = f // 4, (f % 4) * 128
                for half in (0, 4):
                    S.dma("gpsimd", lambda e, g=g, rr=rr, half=half, par=par:
                          e.dma_start(
                              out=contrib_fwd[(g + half) * FWD_ROWS + rr:
                                              (g + half) * FWD_ROWS + rr + 128, :],
                              in_=ab("evb" + par, 0, T)),
                          reads=bkeys("evb" + par, 0, T),
                          writes=["cfwd_dt%d_%d" % (f, half)])

        if "p1" in stages:
            S.dma("sync", lambda e: e.dma_start(
                out=af("wstA", 0, 512).rearrange("p (c m) -> p c m", m=32),
                in_=WXT.rearrange("(c p) m -> p c m", p=128)),
                writes=fkeys("wstA", 0, 512))
            S.op("scalar", lambda e: e.copy(ab("wbfA", 0, 512),
                                            af("wstA", 0, 512)),
                 reads=fkeys("wstA", 0, 512), writes=bkeys("wbfA", 0, 512))

            def mmx(e):
                last = None
                for k in range(NI):
                    lhs = ab("wbfA", k * 32, 32)
                    for ts in range(NTS):
                        rhs = ARB[:, AB["xc"] + k * T + ts * TS:
                                  AB["xc"] + k * T + (ts + 1) * TS]
                        last = e.matmul(pbank[ts][0:32, :], lhs, rhs,
                                        start=(k == 0), stop=(k == NI - 1))
                return last
            S.op("tensor", mmx, reads=bkeys("wbfA", 0, 512) + bkeys("xc", 0, NI * T),
                 writes=["pb0", "pb1", "pb2", "pb3"])
            for ts in range(NTS):
                S.op("vector", lambda e, ts=ts: e.tensor_copy(
                    ab("bcb", ts * TS, TS, 0, 32), pbank[ts][0:32, :]),
                    reads=[PB[ts]], writes=bkeys("bcb", 0, T))
            for gd in range(8):
                S.dma("gpsimd", lambda e, gd=gd: e.dma_start(
                    out=contrib_fwd[gd * FWD_ROWS + 1024: gd * FWD_ROWS + 1056, :]
                        .rearrange("(s two) t -> two s t", two=2),
                    in_=ab("bcb", 0, T, 0, 32)),
                    reads=bkeys("bcb", 0, T), writes=["cfwd_bc%d" % gd])

        if "p2" in stages:
            def cfw(e):
                with allow_ici_all_to_all():
                    return e.collective_compute(
                        "AllToAll", ALU.bypass, replica_groups=[list(range(8))],
                        ins=[contrib_fwd[:, :]], outs=[a2a_fwd[:, :]])
            cfwd_keys = (["cfwd_u%d_%d" % (f, h) for f in range(NI) for h in (0, 4)]
                         + ["cfwd_dt%d_%d" % (f, h) for f in range(NI) for h in (0, 4)]
                         + ["cfwd_bc%d" % gd for gd in range(8)])
            S.op("gpsimd", cfw, reads=cfwd_keys, writes=["a2af"], sem="cfw", amt=1)

        def carry_ap(j):
            return ARB[:, AB["carry"] + j: AB["carry"] + j + 1]

        if "p3" in stages:
            n_w_done = 0
            for w in range(NW):
                for ib in range(4):
                    S.dma("gpsimd", lambda e, w=w, ib=ib: e.indirect_dma_start(
                        out=ab("dtw", ib * T, T), out_offset=None,
                        in_=a2a_fwd[:, :],
                        in_offset=IndirectOffsetOnAxis(
                            gidx_s[:, w * 9 + ib:w * 9 + ib + 1], 0)),
                        reads=["a2af", "gidx"], writes=["dtw%d" % ib])
                    S.dma("gpsimd", lambda e, w=w, ib=ib: e.indirect_dma_start(
                        out=ab("uw", ib * T, T), out_offset=None,
                        in_=a2a_fwd[:, :],
                        in_offset=IndirectOffsetOnAxis(
                            gidx_s[:, w * 9 + 4 + ib:w * 9 + 4 + ib + 1], 0)),
                        reads=["a2af", "gidx"], writes=["uw%d" % ib])
                for ib in range(4):
                    S.op("vector", lambda e, ib=ib: e.tensor_mul(
                        ab("dtuw", ib * T, T), ab("dtw", ib * T, T),
                        ab("uw", ib * T, T)),
                        reads=["dtw%d" % ib, "uw%d" % ib],
                        writes=["dtu%d" % ib])

                def emit_repl(w, pr):
                    par2 = pr % 2
                    gc = 52 + w * 8 + pr
                    S.dma("gpsimd", lambda e, gc=gc, par2=par2: e.indirect_dma_start(
                        out=ab("bc2", par2 * 8192, 8192), out_offset=None,
                        in_=a2a_fwd[:, :],
                        in_offset=IndirectOffsetOnAxis(gidx_s[:, gc:gc + 1], 0)),
                        reads=["a2af", "gidx"], writes=["bc2_%d" % par2])

                for ib in range(4):
                    emit_repl(w, 0)
                    emit_repl(w, 1)
                    for s in range(16):
                        if s % 2 == 0 and s // 2 + 2 < 8:
                            emit_repl(w, s // 2 + 2)
                        par2 = (s // 2) % 2
                        boff = par2 * 8192 + (s % 2) * 4096
                        sp = s % 2
                        hp = s % 2
                        dqi = sp * T
                        dk = ["dax%d" % sp]
                        bk2 = ["btx%d" % sp]
                        hk = ["h2_%d" % hp]
                        S.op("scalar", lambda e, s=s, ib=ib, dqi=dqi:
                             e.activation(ab("dax", dqi, T),
                                          ab("dtw", ib * T, T),
                                          AF.Exp, scale=alog_s[:, s:s + 1]),
                             reads=["dtw%d" % ib, "alog"], writes=dk)
                        S.op("vector", lambda e, ib=ib, dqi=dqi, boff=boff:
                             e.tensor_mul(ab("btx", dqi, T),
                                          ab("dtuw", ib * T, T),
                                          ab("bc2", boff, T)),
                             reads=["dtu%d" % ib, "bc2_%d" % par2],
                             writes=bk2)
                        j = s * 4 + ib
                        init = 0.0 if n_w_done == 0 else carry_ap(j)
                        S.op("vector", lambda e, dqi=dqi, hp=hp, init=init:
                             e.tensor_tensor_scan(
                                 out=ab("h2", hp * T, T),
                                 data0=ab("dax", dqi, T),
                                 data1=ab("btx", dqi, T),
                                 initial=init, op0=ALU.mult, op1=ALU.add),
                             reads=(dk + bk2
                                    + ([] if n_w_done == 0 else bkeys("carry", 0, 64))),
                             writes=hk)
                        S.op("vector", lambda e, j=j, hp=hp: e.tensor_copy(
                            ARB[:, AB["carry"] + j: AB["carry"] + j + 1],
                            ab("h2", hp * T + T - 1, 1)),
                            reads=hk, writes=bkeys("carry", 0, 64))
                        S.op("gpsimd", lambda e, hp=hp, boff=boff:
                             e.tensor_mul(ab("hc2", hp * T, T),
                                          ab("h2", hp * T, T),
                                          ab("bc2", boff + 2048, T)),
                             reads=hk + ["bc2_%d" % par2],
                             writes=["hc2_%d" % hp])

                        def seg(e, s=s, hp=hp):
                            last = None
                            for cc in range(4):
                                last = e.matmul(
                                    pbank[cc][:, :], id_s[:, :],
                                    ab("hc2", hp * T + cc * CC, CC),
                                    start=(s == 0), stop=(s == 15))
                            return last
                        S.op("tensor", seg,
                             reads=["hc2_%d" % hp, "id"],
                             writes=["pb0", "pb1", "pb2", "pb3"])
                    for cc in range(4):
                        S.op("vector", lambda e, ib=ib, cc=cc:
                             e.scalar_tensor_tensor(
                                 ab("dtuw", ib * T + cc * CC, CC),
                                 ab("uw", ib * T + cc * CC, CC),
                                 dvec_s[:, ib:ib + 1], pbank[cc][:, :],
                                 ALU.mult, ALU.add),
                             reads=["uw%d" % ib, "dvec", PB[cc]],
                             writes=["dtu%d" % ib])
                n_w_done += 1
                for ib in range(4):
                    for half in (0, 4):
                        S.dma("gpsimd", lambda e, w=w, ib=ib, half=half: e.dma_start(
                            out=contrib_y[(w + half) * CH + ib * 128:
                                          (w + half) * CH + (ib + 1) * 128, :],
                            in_=ab("dtuw", ib * T, T)),
                            reads=["dtu%d" % ib],
                            writes=["cy_%d_%d_%d" % (w, ib, half)])

        if "p4" in stages:
            def cy(e):
                with allow_ici_all_to_all():
                    return e.collective_compute(
                        "AllToAll", ALU.bypass, replica_groups=[list(range(8))],
                        ins=[contrib_y[:, :]], outs=[a2a_y[:, :]])
            cy_keys = ["cy_%d_%d_%d" % (w, ib, h) for w in range(NW)
                       for ib in range(4) for h in (0, 4)]
            S.op("gpsimd", cy, reads=cy_keys, writes=["a2ay"], sem="cy", amt=1)

        if "p5" in stages:
            for f in range(NI):
                gs = f // 4
                isub = f % 4
                S.dma("gpsimd", lambda e, gs=gs, isub=isub: e.indirect_dma_start(
                    out=ab("ytile", 0, T), out_offset=None, in_=a2a_y[:, :],
                    in_offset=IndirectOffsetOnAxis(
                        gidx_s[:, 36 + gs * 4 + isub:37 + gs * 4 + isub], 0)),
                    reads=["a2ay", "gidx"], writes=bkeys("ytile", 0, T))
                S.dma("sync", lambda e, f=f: e.dma_start(
                    out=ab("rstile", 0, T), in_=res_d[f * 128:(f + 1) * 128, :]),
                    reads=["resd%d" % f], writes=bkeys("rstile", 0, T))
                S.op("scalar", lambda e: e.activation(
                    ab("silur", 0, T), ab("rstile", 0, T), AF.Silu),
                    reads=bkeys("rstile", 0, T), writes=bkeys("silur", 0, T))
                S.op("vector", lambda e, f=f: e.tensor_mul(
                    ab("gated", f * T, T), ab("ytile", 0, T), ab("silur", 0, T)),
                    reads=bkeys("ytile", 0, T) + bkeys("silur", 0, T),
                    writes=bkeys("gated", f * T, T))

            for ts in range(NTS):
                for m in range(ND):
                    par = "CD"[m % 2]
                    S.dma("sync", lambda e, m=m, par=par: e.dma_start(
                        out=af("wst" + ("A" if par == "C" else "B"), 0, DI)
                            .rearrange("p (c m) -> p c m", m=128),
                        in_=WOUTT[:, m * 128:(m + 1) * 128]
                            .rearrange("(c p) m -> p c m", p=128)),
                        writes=fkeys("wst" + ("A" if par == "C" else "B"), 0, DI))
                    S.op("scalar", lambda e, par=par: e.copy(
                        ab("wbf" + par, 0, DI),
                        af("wst" + ("A" if par == "C" else "B"), 0, DI)),
                        reads=fkeys("wst" + ("A" if par == "C" else "B"), 0, DI),
                        writes=bkeys("wbf" + par, 0, DI))

                    def mmo(e, ts=ts, m=m, par=par):
                        last = None
                        for k in range(NI):
                            last = e.matmul(
                                pbank[m % 2][:, :], ab("wbf" + par, k * 128, 128),
                                ARB[:, AB["gated"] + k * T + ts * TS:
                                    AB["gated"] + k * T + (ts + 1) * TS],
                                start=(k == 0), stop=(k == NI - 1))
                        return last
                    S.op("tensor", mmo,
                         reads=bkeys("wbf" + par, 0, DI) + bkeys("gated", 0, NI * T),
                         writes=[PB[m % 2]])
                    S.op("vector", lambda e, m=m: e.tensor_copy(
                        ab("outcm", m * TS, TS), pbank[m % 2][:, :]),
                        reads=[PB[m % 2]], writes=bkeys("outcm", m * TS, TS))

                for tsub in range(4):
                    tt = ts * 4 + tsub
                    parx = "AB"[tt % 2]
                    xa = af("x0" + parx, 0, D)
                    xk = fkeys("x0" + parx, 0, D)
                    S.dma("sync", lambda e, tt=tt, xa=xa: e.dma_start(
                        out=xa, in_=X[tt * 128:(tt + 1) * 128, :]), writes=xk)

                    def tro(e, tsub=tsub):
                        last = None
                        for m in range(ND):
                            last = e.transpose(
                                pt[:, m * 128:(m + 1) * 128],
                                ARB[:, AB["outcm"] + m * TS + tsub * 128:
                                    AB["outcm"] + m * TS + (tsub + 1) * 128],
                                id_s[:, :])
                        return last
                    S.op("tensor", tro, reads=bkeys("outcm", 0, ND * TS) + ["id"],
                         writes=["pt"])
                    emit_ln(xa, xk, ab("ytile", 0, D), bkeys("ytile", 0, D))
                    S.op("vector", lambda e: e.tensor_add(
                        af("zt", 0, D), pt[:, 0:D], ab("ytile", 0, D)),
                        reads=["pt"] + bkeys("ytile", 0, D), writes=fkeys("zt", 0, D))
                    emit_ln(af("zt", 0, D), fkeys("zt", 0, D),
                            af("mt", 0, D), fkeys("mt", 0, D))
                    S.op("vector", lambda e: e.tensor_mul(
                        af("at2", 0, D), af("mt", 0, D), gb_s[:, 0:D]),
                        reads=fkeys("mt", 0, D) + ["gb"], writes=fkeys("at2", 0, D))
                    S.op("vector", lambda e: e.tensor_add(
                        af("at2", 0, D), af("at2", 0, D), gb_s[:, D:2 * D]),
                        reads=fkeys("at2", 0, D) + ["gb"], writes=fkeys("at2", 0, D))
                    S.op("vector", lambda e, xa=xa: e.tensor_add(
                        af("x1t", 0, D), af("at2", 0, D), xa),
                        reads=fkeys("at2", 0, D) + xk, writes=fkeys("x1t", 0, D))
                    S.dma("gpsimd", lambda e, tt=tt: e.dma_start(
                        out=x1_d[tt * 128:(tt + 1) * 128, :], in_=af("x1t", 0, D)),
                        reads=fkeys("x1t", 0, D), writes=["x1d%d" % tt])
                    emit_ln(af("x1t", 0, D), fkeys("x1t", 0, D),
                            ab("ytile", 0, D), bkeys("ytile", 0, D))

                    def trh(e):
                        last = None
                        for j in range(ND):
                            last = e.transpose(pt[:, j * 128:(j + 1) * 128],
                                               ab("ytile", j * 128, 128), id_s[:, :])
                        return last
                    S.op("tensor", trh, reads=bkeys("ytile", 0, D) + ["id"],
                         writes=["pt"])

                    def evh(e, tt=tt):
                        dst = ARB[:, AB["hT"]: AB["hT"] + ND * T] \
                            .rearrange("p (c m) -> p c m", m=T)[:, :, tt * 128:(tt + 1) * 128]
                        src = pt[:, 0:1024].rearrange("p (c m) -> p c m", m=128)
                        return e.tensor_copy(dst, src)
                    S.op("vector", evh, reads=["pt"], writes=bkeys("hT", 0, ND * T))

                    def mmr(e, tt=tt):
                        last = None
                        for k in range(ND):
                            last = e.matmul(
                                pbank[6][:, 0:7],
                                ARB[:, AB["hT"] + k * T + tt * 128:
                                    AB["hT"] + k * T + (tt + 1) * 128],
                                rwT_s[:, k * 7:(k + 1) * 7],
                                start=(k == 0), stop=(k == ND - 1))
                        return last
                    S.op("tensor", mmr, reads=bkeys("hT", 0, ND * T) + ["rwT"],
                         writes=["pb6"])
                    S.op("vector", lambda e: e.tensor_copy(lg_t[:, 0:7], pbank[6][:, 0:7]),
                         reads=["pb6"], writes=["lg"])
                    S.op("vector", lambda e: e.tensor_add(lg_t[:, 0:7], lg_t[:, 0:7],
                                                          rbias_s[:, 0:7]),
                         reads=["lg", "rbias"], writes=["lg"])
                    S.op("vector", lambda e: e.tensor_scalar(
                        lg_t[:, 0:7], lg_t[:, 0:7], -50.0, 50.0, ALU.max, ALU.min),
                        reads=["lg"], writes=["lg"])
                    S.op("scalar", lambda e: e.activation(prb[:, 0:7], lg_t[:, 0:7],
                                                          AF.Sigmoid),
                         reads=["lg"], writes=["prb"])
                    S.op("vector", lambda e: e.reduce_max(pmax[:, :], prb[:, 0:7], axis=AX),
                         reads=["prb"], writes=["pmax"])
                    S.op("vector", lambda e: e.tensor_scalar(
                        pm7[:, 0:7], ones7[:, 0:7], pmax[:, 0:1], None, ALU.mult),
                        reads=["ones7", "pmax"], writes=["pm7"])
                    S.op("vector", lambda e: e.tensor_tensor(
                        equ[:, 0:7], prb[:, 0:7], pm7[:, 0:7], ALU.is_equal),
                        reads=["prb", "pm7"], writes=["equ"])
                    S.op("vector", lambda e: e.select(
                        eff[:, 0:7], equ[:, 0:7], ones7[:, 0:7], zero7[:, 0:7]),
                        reads=["equ", "ones7", "zero7"], writes=["eff"])
                    S.op("vector", lambda e: e.tensor_tensor_scan(
                        out=pre[:, 0:7], data0=ones7[:, 0:7], data1=eff[:, 0:7],
                        initial=0.0, op0=ALU.mult, op1=ALU.add),
                        reads=["ones7", "eff"], writes=["pre"])
                    S.op("vector", lambda e: e.tensor_tensor(
                        msk[:, 0:7], pre[:, 0:7], eff[:, 0:7], ALU.subtract),
                        reads=["pre", "eff"], writes=["msk"])
                    S.op("vector", lambda e: e.tensor_tensor(
                        msk[:, 0:7], eff[:, 0:7], msk[:, 0:7], ALU.subtract),
                        reads=["eff", "msk"], writes=["msk"])
                    S.op("vector", lambda e: e.tensor_scalar(
                        msk[:, 0:7], msk[:, 0:7], 0.0, None, ALU.max),
                        reads=["msk"], writes=["msk"])
                    S.op("vector", lambda e: e.tensor_scalar(
                        pclp[:, 0:1], pmax[:, 0:1], 1e-8, 1.0 - 1e-8, ALU.max, ALU.min),
                        reads=["pmax"], writes=["pclp"])
                    S.op("vector", lambda e: e.tensor_scalar(
                        pclp[:, 1:2], pclp[:, 0:1], 0.5, 1e-8, ALU.add, ALU.add),
                        reads=["pclp"], writes=["pclp"])
                    S.op("vector", lambda e: e.tensor_scalar(
                        pclp[:, 1:2], pclp[:, 1:2], 0.5, 2.0, ALU.max, ALU.min),
                        reads=["pclp"], writes=["pclp"])
                    S.op("vector", lambda e: e.reciprocal(pclp[:, 2:3], pclp[:, 1:2]),
                         reads=["pclp"], writes=["pclp"])
                    S.op("vector", lambda e, tt=tt: e.tensor_scalar(
                        w8cat[:, tt * 8:tt * 8 + 7], msk[:, 0:7], pclp[:, 0:1],
                        pclp[:, 2:3], ALU.mult, ALU.mult),
                        reads=["msk", "pclp"], writes=["w8c"])
                    S.op("vector", lambda e, tt=tt: e.tensor_scalar(
                        w8cat[:, tt * 8 + 7:tt * 8 + 8], pclp[:, 2:3], 0.5, None,
                        ALU.mult),
                        reads=["pclp"], writes=["w8c"])

            S.op("vector", lambda e: e.tensor_copy(w8b[:, :], w8cat[:, :]),
                 reads=["w8c"], writes=["w8b"])
            S.op("tensor", lambda e: e.transpose(pt[:, 0:128], w8b[:, :], id_s[:, :]),
                 reads=["w8b", "id"], writes=["pt"])
            S.op("vector", lambda e: e.tensor_copy(w8b[:, :], pt[:, 0:128]),
                 reads=["pt"], writes=["w8b"])
            S.dma("gpsimd", lambda e: e.dma_start(
                out=w8r_d.rearrange("(e8 tt) t -> tt e8 t", tt=16),
                in_=w8b[:, :]),
                reads=["w8b"], writes=["w8d"])

        if "p5" in stages:
            for tsp in range(2):
                tok0 = tsp * 1024
                for e_i in range(E + 1):
                    S.dma("gpsimd", lambda e, e_i=e_i, tsp=tsp: e.indirect_dma_start(
                        out=ab("wbc", 0, 1024), out_offset=None,
                        in_=w8r_d[:, :],
                        in_offset=IndirectOffsetOnAxis(
                            gidx_s[:, 84 + e_i * 2 + tsp:85 + e_i * 2 + tsp], 0)),
                        reads=["w8d", "gidx"], writes=bkeys("wbc", 0, 1024))

                    for mb in range(11):
                        par = "AB"[mb % 2]
                        wstk = fkeys("wstA", 0, 4096)
                        w13k = bkeys("w13" + par, 0, 4096)
                        for hh in range(2):
                            S.dma("sync", lambda e, e_i=e_i, mb=mb, hh=hh: e.dma_start(
                                out=af("wstA", hh * 2048, 2048)
                                    .rearrange("p (c m) -> p c m", m=512),
                                in_=W13R[e_i, mb, hh * 512:(hh + 1) * 512, :]
                                    .rearrange("(c p) m -> p c m", p=128)),
                                writes=wstk)
                        S.op("scalar", lambda e, par=par: e.copy(
                            ab("w13" + par, 0, 4096), af("wstA", 0, 4096)),
                            reads=wstk, writes=w13k)

                        for mi in range(2):
                            m = mb * 2 + mi
                            def mg(e, par=par, mi=mi, tok0=tok0, woff=0, p0=0):
                                last = None
                                for k in range(ND):
                                    lhs = ARB[:, AB["w13" + par] + k * 512 + woff + mi * 128:
                                              AB["w13" + par] + k * 512 + woff + (mi + 1) * 128]
                                    for t2 in range(2):
                                        rhs = ARB[:, AB["hT"] + k * T + tok0 + t2 * TS:
                                                  AB["hT"] + k * T + tok0 + (t2 + 1) * TS]
                                        last = e.matmul(pbank[p0 + t2][:, :], lhs, rhs,
                                                        start=(k == 0), stop=(k == ND - 1))
                                return last
                            S.op("tensor", lambda e, par=par, mi=mi, tok0=tok0:
                                 mg(e, par, mi, tok0, 0, 0),
                                 reads=w13k + bkeys("hT", 0, ND * T),
                                 writes=["pb0", "pb1"])
                            for t2 in range(2):
                                S.op("scalar", lambda e, m=m, t2=t2: e.activation(
                                    ab("gg", m * 1024 + t2 * TS, TS),
                                    pbank[t2][:, :], AF.Silu),
                                    reads=[PB[t2]],
                                    writes=bkeys("gg", m * 1024, 1024))
                            S.op("tensor", lambda e, par=par, mi=mi, tok0=tok0:
                                 mg(e, par, mi, tok0, 256, 2),
                                 reads=w13k + bkeys("hT", 0, ND * T),
                                 writes=["pb2", "pb3"])
                            so_ = 8192 + (mi % 2) * 1024
                            for t2 in range(2):
                                S.op("vector", lambda e, t2=t2, so_=so_:
                                     e.scalar_tensor_tensor(
                                         ab("w2h", so_ + t2 * TS, TS),
                                         pbank[2 + t2][:, :], 1.0,
                                         ab("wbc", t2 * TS, TS),
                                         ALU.mult, ALU.mult),
                                     reads=[PB[2 + t2]] + bkeys("wbc", 0, 1024),
                                     writes=bkeys("w2h", so_, 1024))
                            for t2 in range(2):
                                S.op("gpsimd", lambda e, m=m, t2=t2, so_=so_:
                                     e.tensor_mul(
                                         ab("gg", m * 1024 + t2 * TS, TS),
                                         ab("gg", m * 1024 + t2 * TS, TS),
                                         ab("w2h", so_ + t2 * TS, TS)),
                                     reads=bkeys("w2h", so_, 1024)
                                           + bkeys("gg", m * 1024, 1024),
                                     writes=bkeys("gg", m * 1024, 1024))

                    for kh in range(2):
                        kn = 11
                        for kk in range(kn):
                            k = kh * 11 + kk
                            par = "AB"[kk % 2]
                            S.dma("sync", lambda e, e_i=e_i, k=k, par=par: e.dma_start(
                                out=af("wst" + par, 0, D),
                                in_=W2T[e_i, k * 128:(k + 1) * 128, :]),
                                writes=fkeys("wst" + par, 0, D))
                            S.op("vector", lambda e, k=k, kk=kk, par=par: e.tensor_copy(
                                ab("w2h", kk * 1024, 1024), af("wst" + par, 0, D)),
                                reads=fkeys("wst" + par, 0, D),
                                writes=bkeys("w2h", kk * 1024, 1024))
                        for tt2 in range(8):
                            def mw2(e, kh=kh, tt2=tt2, tok0=tok0):
                                last = None
                                for kk in range(11):
                                    k = kh * 11 + kk
                                    lhs = ARB[:, AB["gg"] + k * 1024 + tt2 * 128:
                                              AB["gg"] + k * 1024 + (tt2 + 1) * 128]
                                    for n in range(2):
                                        last = e.matmul(
                                            pbank[4 + n][:, :],
                                            lhs,
                                            ARB[:, AB["w2h"] + kk * 1024 + n * 512:
                                                AB["w2h"] + kk * 1024 + (n + 1) * 512],
                                            start=(kk == 0), stop=(kk == 10))
                                return last
                            S.op("tensor", mw2,
                                 reads=bkeys("gg", 0, NII * 1024) + bkeys("w2h", 0, 11 * 1024),
                                 writes=["pb4", "pb5"])
                            tt = tsp * 8 + tt2
                            for n in range(2):
                                if e_i == 0 and kh == 0:
                                    S.op("vector", lambda e, tt=tt, n=n: e.tensor_copy(
                                        ab("acc", tt * 1024 + n * 512, 512),
                                        pbank[4 + n][:, :]),
                                        reads=[PB[4 + n]],
                                        writes=bkeys("acc", tt * 1024, 1024))
                                else:
                                    S.op("vector", lambda e, tt=tt, n=n: e.tensor_add(
                                        ab("acc", tt * 1024 + n * 512, 512),
                                        ab("acc", tt * 1024 + n * 512, 512),
                                        pbank[4 + n][:, :]),
                                        reads=[PB[4 + n]] + bkeys("acc", tt * 1024, 1024),
                                        writes=bkeys("acc", tt * 1024, 1024))

            for tt in range(NTT):
                parx = "AB"[tt % 2]
                xa = af("x0" + parx, 0, D)
                xk = fkeys("x0" + parx, 0, D)
                S.dma("sync", lambda e, tt=tt, xa=xa: e.dma_start(
                    out=xa, in_=x1_d[tt * 128:(tt + 1) * 128, :]),
                    reads=["x1d%d" % tt], writes=xk)
                S.op("vector", lambda e, tt=tt, xa=xa: e.tensor_add(
                    af("zt", 0, D), ab("acc", tt * 1024, 1024), xa),
                    reads=bkeys("acc", tt * 1024, 1024) + xk,
                    writes=fkeys("zt", 0, D))
                S.dma("gpsimd", lambda e, tt=tt: e.dma_start(
                    out=OUT[tt * 128:(tt + 1) * 128, :], in_=af("zt", 0, D)),
                    reads=fkeys("zt", 0, D), writes=["outd"])

        if "res" in debug:
            DR = dbg("res", (DI, T), BF16)
            S.dma("gpsimd", lambda e: e.dma_start(out=DR, in_=res_d[:, :]),
                  reads=["resd%d" % i for i in range(NI)], writes=["dbgr"])
        if "cfwd" in debug:
            DF = dbg("cfwd", (8 * FWD_ROWS, T), BF16)
            S.dma("gpsimd", lambda e: e.dma_start(out=DF, in_=contrib_fwd[:, :]),
                  reads=(["cfwd_u%d_%d" % (f, h) for f in range(NI) for h in (0, 4)]
                         + ["cfwd_dt%d_%d" % (f, h) for f in range(NI) for h in (0, 4)]
                         + ["cfwd_bc%d" % gd for gd in range(8)]), writes=["dbgf"])
        if "a2af" in debug:
            DA2 = dbg("a2af", (8 * FWD_ROWS, T), BF16)
            S.dma("gpsimd", lambda e: e.dma_start(out=DA2, in_=a2a_fwd[:, :]),
                  reads=["a2af"], writes=["dbga"])
        if "ydbg" in debug:
            DY = dbg("ydbg", (8 * CH, T), BF16)
            S.dma("gpsimd", lambda e: e.dma_start(out=DY, in_=contrib_y[:, :]),
                  reads=["cy_%d_%d_%d" % (w, ib, h) for w in range(NW)
                         for ib in range(4) for h in (0, 4)], writes=["dbgy"])
        if "a2ay" in debug:
            DY2 = dbg("a2ay", (8 * CH, T), BF16)
            S.dma("gpsimd", lambda e: e.dma_start(out=DY2, in_=a2a_y[:, :]),
                  reads=["a2ay"], writes=["dbgy2"])
        if "x1" in debug:
            DX1 = dbg("x1", (T, D))
            S.dma("gpsimd", lambda e: e.dma_start(out=DX1, in_=x1_d[:, :]),
                  reads=["x1d%d" % tt for tt in range(NTT)], writes=["dbgx1"])

        final = dict(S.cnt)
        S.op("sync", lambda e: e.nop(),
             deps=[("ld", final.get("ld", 0))])
        S.op("gpsimd", lambda e: e.nop(),
             deps=[("gq", final.get("gq", 0)), ("aq", final.get("aq", 0))])

        @block.sync
        def _(sync):
            S.emit_engine("sync", sync, sems)

        @block.tensor
        def _(tensor):
            S.emit_engine("tensor", tensor, sems)

        @block.vector
        def _(vector):
            S.emit_engine("vector", vector, sems)

        @block.scalar
        def _(scalar):
            S.emit_engine("scalar", scalar, sems)

        @block.gpsimd
        def _(gpsimd):
            S.emit_engine("gpsimd", gpsimd, sems)

    return nc, dbg_outs


def kernel(**inputs):
    shared, per_core = _host_prep(inputs)
    nc, _ = build_program()
    in_maps = []
    for c in range(8):
        m = dict(shared)
        m.update(per_core[c])
        in_maps.append(m)
    res = run_bass_kernel_spmd(nc, in_maps, core_ids=list(range(8)))
    out = np.empty((B, L, D), np.float32)
    for c in range(8):
        b, g = c // 4, c % 4
        out[b, g * T:(g + 1) * T] = np.asarray(res.results[c]["out"], np.float32)
    return out


# version 12
# speedup vs baseline: 1.1276x; 1.1276x over previous
import numpy as np
import ml_dtypes
from contextlib import ExitStack

import concourse.bass as bass
import concourse.mybir as mybir
from concourse.bass_utils import run_bass_kernel_spmd
from concourse.bass_core_rust import IndirectOffsetOnAxis
from concourse.aot_env import allow_ici_all_to_all

F32 = mybir.dt.float32
BF16 = mybir.dt.bfloat16
U32 = mybir.dt.uint32
AF = mybir.ActivationFunctionType
ALU = mybir.AluOpType
AX = mybir.AxisListType.X

B, L, D = 2, 8192, 1024
T = 2048
DI, DS, II, E = 2048, 16, 2816, 7
ND = D // 128
NF = 4096 // 128
NI = DI // 128
NII = II // 128
NTT = T // 128
TS = 512
NTS = T // TS
CH = DI // 4
NW = 4
CC = 512
FWD_ROWS = 1056
LN_EPS = 1e-5
NGC = 100
BS = 2048

ARB_N = 67584
ARF_N = 14336
AB = {
    "xc": 0, "aT": 32768, "wbfA": 49152, "wbfB": 51200,
    "xpcA": 53248, "xpcB": 55304, "evbA": 57360, "evbB": 59408,
    "bcb": 61456, "caccb": 63504,
    "dtw": 0, "uw": 8192, "dtuw": 16384, "h2": 24576,
    "dax": 28672, "btx": 32768, "bc2": 36864,
    "hc2": 53248, "carry": 57344,
    "hT": 0, "gated": 16384, "outcm": 49152, "ytile": 53248,
    "rstile": 55296, "silur": 57344, "wbfC": 59392, "wbfD": 61440,
    "gg": 16384, "w2h": 38912, "acc": 50176, "wbc": 66560,
    "w13A": 38912, "w13B": 43008,
}
AFm = {
    "xtA": 0, "xtB": 1024, "xh": 2048, "scr1": 3072, "x0A": 4096,
    "x0B": 5120, "x1t": 6144, "zt": 7168, "mt": 8192, "at2": 9216,
    "wstA": 10240, "wstB": 12288,
}


def _host_prep(inputs):
    f32 = np.float32
    asc = np.ascontiguousarray
    x = np.asarray(inputs["L0_attn_x"], f32)
    win = np.asarray(inputs["L0_attn_in_proj_w"], f32)
    convw = np.asarray(inputs["L0_attn_conv_w"], f32)[:, 0, :]
    convb = np.asarray(inputs["L0_attn_conv_b"], f32)
    wx = np.asarray(inputs["L0_attn_x_proj_w"], f32)
    wdt = np.asarray(inputs["L0_attn_dt_proj_w"], f32)
    dtb = np.asarray(inputs["L0_attn_dt_proj_b"], f32)
    alog = np.asarray(inputs["L0_attn_A_log"], f32)
    dvec = np.asarray(inputs["L0_attn_D"], f32)
    wout = np.asarray(inputs["L0_attn_out_proj_w"], f32)
    gamma = np.asarray(inputs["L0_attn_ln_gamma"], f32)
    beta = np.asarray(inputs["L0_attn_ln_beta"], f32)
    rw = np.asarray(inputs["L0_ffn_router_w"], f32)
    rb = np.asarray(inputs["L0_ffn_routing_bias"], f32)
    sw1 = np.asarray(inputs["L0_ffn_sw1"], f32)
    sw2 = np.asarray(inputs["L0_ffn_sw2"], f32)
    sw3 = np.asarray(inputs["L0_ffn_sw3"], f32)
    ew1 = np.asarray(inputs["L0_ffn_ew1"], f32)
    ew2 = np.asarray(inputs["L0_ffn_ew2"], f32)
    ew3 = np.asarray(inputs["L0_ffn_ew3"], f32)

    oh = np.zeros((32, 32, 128), dtype=ml_dtypes.bfloat16)
    for j in range(32):
        oh[:, j, :][j, :] = 1.0
    shared = {
        "winT": asc(win.T),
        "wxT": asc(wx.T),
        "wdtT": asc(wdt.T),
        "woutT": asc(wout.T),
        "convw_r": asc(convw.reshape(NI, 128, 4).transpose(1, 0, 2).reshape(128, NI * 4)),
        "convb_r": asc(convb.reshape(NI, 128).T),
        "dtb_r": asc(dtb.reshape(NI, 128).T),
        "alog_r": asc(np.tile(alog, (128, 1))),
        "gamma_r": asc(gamma.reshape(1, D)),
        "beta_r": asc(beta.reshape(1, D)),
        "rwT": asc(rw.T),
        "rbias_r": asc(np.tile(rb, (128, 1))),
        "w1T": asc(np.concatenate([ew1.transpose(0, 2, 1), sw1.T[None]], 0)),
        "w3T": asc(np.concatenate([ew3.transpose(0, 2, 1), sw3.T[None]], 0)),
        "w2T": asc(np.concatenate([ew2.transpose(0, 2, 1), sw2.T[None]], 0)),
        "ident_b": np.eye(128, dtype=ml_dtypes.bfloat16),
        "ones_b": np.ones((1, 128), dtype=ml_dtypes.bfloat16),
        "ones_f": np.ones((1, 128), dtype=f32),
        "onehot": asc(oh.reshape(32, 32 * 128)),
    }

    w1b = shared["w1T"].reshape(8, 1024, 11, 256)
    w3b = shared["w3T"].reshape(8, 1024, 11, 256)
    shared["w13R"] = asc(np.concatenate([w1b, w3b], axis=-1).transpose(0, 2, 1, 3))
    del shared["w1T"], shared["w3T"]

    per_core = []
    for c in range(8):
        b, g = c // 4, c % 4
        xs = asc(x[b, g * T:(g + 1) * T])
        halo = np.zeros((3, D), f32) if g == 0 else asc(x[b, g * T - 3:g * T])
        dvec_my = asc(dvec.reshape(NI, 128).T[:, g * 4:(g + 1) * 4])
        gidx = np.zeros((128, NGC), np.uint32)
        base = 0 if c < 4 else 4 * FWD_ROWS
        ybase = 0 if c < 4 else 4 * CH
        r = np.arange(128, dtype=np.uint32)
        for w in range(NW):
            for ib in range(4):
                gidx[:, w * 9 + ib] = base + w * FWD_ROWS + ib * 128 + r
                gidx[:, w * 9 + 4 + ib] = base + w * FWD_ROWS + 512 + ib * 128 + r
            gidx[0:32, w * 9 + 8] = base + w * FWD_ROWS + 1024 + np.arange(32, dtype=np.uint32)
        for gs in range(4):
            for isub in range(4):
                gidx[:, 36 + gs * 4 + isub] = ybase + gs * CH + isub * 128 + r
        for w in range(NW):
            for sp2 in range(8):
                gidx[:, 52 + w * 8 + sp2] = base + w * FWD_ROWS + 1024 + 4 * sp2
        for e_i in range(8):
            for tsp in range(2):
                gidx[:, 84 + e_i * 2 + tsp] = e_i * 16 + tsp * 8
        per_core.append({"x": xs, "xhalo": halo, "dvec_my": dvec_my, "gidx": gidx})
    return shared, per_core


class Sched:
    ENG_SEM = {"tensor": "tv", "vector": "vv", "scalar": "sv",
               "gpsimd": "pv", "sync": "yv"}

    def __init__(self):
        self.ops = []
        self.cnt = {}
        self.last_w = {}
        self.readers = {}

    def _tok(self, sem, amt):
        self.cnt[sem] = self.cnt.get(sem, 0) + amt
        return (sem, self.cnt[sem])

    def op(self, eng, fn, reads=(), writes=(), sem=None, amt=1, deps=()):
        waits = {}

        def need(tok):
            if tok is None:
                return
            s, v = tok
            if waits.get(s, 0) < v:
                waits[s] = v
        for bn in reads:
            need(self.last_w.get(bn))
        for bn in writes:
            need(self.last_w.get(bn))
            for t in self.readers.get(bn, ()):
                need(t)
        for t in deps:
            need(t)
        if sem is None:
            sem = self.ENG_SEM[eng]
        tok = self._tok(sem, amt)
        self.ops.append((eng, waits, fn, sem, amt))
        for bn in writes:
            self.last_w[bn] = tok
            self.readers[bn] = []
        for bn in reads:
            self.readers.setdefault(bn, []).append(tok)
        return tok

    def dma(self, queue, fn, reads=(), writes=(), deps=()):
        qsem = {"sync": "ld", "gpsimd": "gq", "scalar": "aq"}[queue]
        return self.op(queue, fn, reads=reads, writes=writes, sem=qsem,
                       amt=16, deps=deps)

    def emit_engine(self, eng_name, eng, sems):
        wm = {}
        for en, waits, fn, sem, amt in self.ops:
            if en != eng_name:
                continue
            for s, v in sorted(waits.items()):
                if wm.get(s, 0) < v:
                    eng.wait_ge(sems[s], v)
                    wm[s] = v
            inst = fn(eng)
            assert inst is not None
            inst.then_inc(sems[sem], amt)


def bkeys(name, off, n):
    c0 = AB[name] + off
    return ["ab%d" % i for i in range(c0 // BS, -(-(c0 + n) // BS))]


def fkeys(name, off, n):
    c0 = AFm[name] + off
    return ["af%d" % i for i in range(c0 // BS, -(-(c0 + n) // BS))]


def build_program(stages=("p1", "p2", "p3", "p4", "p5"), debug=()):
    nc = bass.Bass("TRN2", target_bir_lowering=False, debug=False)
    S = Sched()

    def dram_in(name, shape, dt=F32):
        return nc.dram_tensor(name, list(shape), dt, kind="ExternalInput").ap()

    X = dram_in("x", (T, D))
    XH = dram_in("xhalo", (3, D))
    WINT = dram_in("winT", (D, 4096))
    WXT = dram_in("wxT", (DI, 32))
    WDTT = dram_in("wdtT", (DI, DI))
    WOUTT = dram_in("woutT", (DI, D))
    CONVW = dram_in("convw_r", (128, NI * 4))
    CONVB = dram_in("convb_r", (128, NI))
    DTB = dram_in("dtb_r", (128, NI))
    ALOG = dram_in("alog_r", (128, NI))
    DVECM = dram_in("dvec_my", (128, 4))
    GAMMA = dram_in("gamma_r", (1, D))
    BETA = dram_in("beta_r", (1, D))
    RWT = dram_in("rwT", (D, 7))
    RBIAS = dram_in("rbias_r", (128, 7))
    W13R = dram_in("w13R", (E + 1, 11, D, 512))
    W2T = dram_in("w2T", (E + 1, II, D))
    IDENT = dram_in("ident_b", (128, 128), BF16)
    ONESB = dram_in("ones_b", (1, 128), BF16)
    ONESF = dram_in("ones_f", (1, 128))
    OH = dram_in("onehot", (32, 32 * 128), BF16)
    GIDX = dram_in("gidx", (128, NGC), U32)

    OUT = nc.dram_tensor("out", [T, D], F32, kind="ExternalOutput").ap()

    contrib_fwd = nc.dram_tensor("contrib_fwd", [8 * FWD_ROWS, T], BF16).ap()
    a2a_fwd = nc.dram_tensor("a2a_fwd", [8 * FWD_ROWS, T], BF16).ap()
    contrib_y = nc.dram_tensor("contrib_y", [8 * CH, T], BF16).ap()
    a2a_y = nc.dram_tensor("a2a_y", [8 * CH, T], BF16).ap()
    res_d = nc.dram_tensor("res_d", [DI, T], BF16).ap()
    x1_d = nc.dram_tensor("x1_d", [T, D], F32).ap()
    w8_d = nc.dram_tensor("w8_d", [128, 128], BF16).ap()
    w8r_d = nc.dram_tensor("w8r_d", [128, 128], BF16).ap()

    dbg_outs = {}

    def dbg(name, shape, dt=F32):
        ap = nc.dram_tensor("dbg_" + name, list(shape), dt, kind="ExternalOutput").ap()
        dbg_outs[name] = ap
        return ap

    with ExitStack() as ctx:
        def sb(name, shape, dt=F32):
            return ctx.enter_context(nc.sbuf_tensor(name, list(shape), dt))

        def psum(name, shape, dt=F32):
            return ctx.enter_context(nc.psum_tensor(name, list(shape), dt))

        ARB = sb("ARB", (128, ARB_N), BF16)
        ARF = sb("ARF", (128, ARF_N))

        def ab(name, off=0, n=None, r0=0, r1=128):
            c0 = AB[name] + off
            return ARB[r0:r1, c0:c0 + n]

        def af(name, off=0, n=None, r0=0, r1=128):
            c0 = AFm[name] + off
            return ARF[r0:r1, c0:c0 + n]

        gidx_s = sb("gidx_s", (128, NGC), U32)
        alog_s = sb("alog_s", (128, NI))
        convw_s = sb("convw_s", (128, NI * 4))
        convb_s = sb("convb_s", (128, NI))
        dtb_s = sb("dtb_s", (128, NI))
        dvec_s = sb("dvec_s", (128, 4))
        rbias_s = sb("rbias_s", (128, 7))
        oh_s = sb("oh_s", (32, 32 * 128), BF16)
        id_s = sb("id_s", (128, 128), BF16)
        onesb_s = sb("onesb_s", (1, 128), BF16)
        onesf_s = sb("onesf_s", (1, 128))
        gb_s = sb("gb_s", (128, 2 * D))
        ones7 = sb("ones7", (128, 8))
        zero7 = sb("zero7", (128, 8))
        rwT_s = sb("rwT_s", (128, ND * 7), BF16)
        aTh = sb("aTh", (128, 24), BF16)
        rwstage = sb("rwstage", (128, ND * 7))
        stat = sb("stat", (128, 16))
        lg_t = sb("lg_t", (128, 8))
        prb = sb("prb", (128, 8))
        pmax = sb("pmax", (128, 1))
        pm7 = sb("pm7", (128, 8))
        equ = sb("equ", (128, 8), U32)
        eff = sb("eff", (128, 8))
        pre = sb("pre", (128, 8))
        msk = sb("msk", (128, 8))
        pclp = sb("pclp", (128, 4))
        w8cat = sb("w8cat", (128, 128))
        w8b = sb("w8b", (128, 128), BF16)

        def wrow_ap(c0, n):
            return ARB[0:1, AB["wbc"] + c0: AB["wbc"] + c0 + n]

        pbank = [psum("pb%d" % i, (128, 512)) for i in range(7)]
        pt = psum("pt", (128, 1024), BF16)
        PB = ["pb%d" % i for i in range(7)]

        sem_names = ["ld", "gq", "aq", "tv", "vv", "sv", "pv", "yv", "cfw", "cy"]
        sems = {n: ctx.enter_context(nc.semaphore(n)) for n in sem_names}
        block = ctx.enter_context(nc.Block())

        for ap_, t_, nm in [(GIDX, gidx_s, "gidx"), (ALOG, alog_s, "alog"),
                            (CONVW, convw_s, "convw"), (CONVB, convb_s, "convb"),
                            (DTB, dtb_s, "dtb"), (DVECM, dvec_s, "dvec"),
                            (RBIAS, rbias_s, "rbias"), (OH, oh_s, "oh"),
                            (IDENT, id_s, "id"), (ONESB, onesb_s, "onesb"),
                            (ONESF, onesf_s, "onesf")]:
            S.dma("sync", lambda e, ap_=ap_, t_=t_: e.dma_start(out=t_[:, :], in_=ap_),
                  writes=[nm])
        S.op("vector", lambda e: e.memset(ones7[:, :], 1.0), writes=["ones7"])
        S.op("vector", lambda e: e.memset(zero7[:, :], 0.0), writes=["zero7"])
        S.dma("sync", lambda e: e.dma_start(out=af("wstA", 0, D, 0, 1), in_=GAMMA),
              writes=fkeys("wstA", 0, 2 * D))
        S.dma("sync", lambda e: e.dma_start(out=af("wstA", D, D, 0, 1), in_=BETA),
              writes=fkeys("wstA", 0, 2 * D))
        for j in range(4):
            S.op("tensor",
                 lambda e, j=j: e.matmul(pbank[4][:, :], onesf_s[0:1, :],
                                         af("wstA", j * 512, 512, 0, 1),
                                         start=True, stop=True),
                 reads=fkeys("wstA", 0, 2 * D) + ["onesf"], writes=["pb4"])
            S.op("vector",
                 lambda e, j=j: e.tensor_copy(gb_s[:, j * 512:(j + 1) * 512],
                                              pbank[4][:, :]),
                 reads=["pb4"], writes=["gb"])
        S.dma("sync", lambda e: e.dma_start(
            out=rwstage[:, :].rearrange("p (c m) -> p c m", m=7),
            in_=RWT.rearrange("(c p) m -> p c m", p=128)),
            writes=["rwst"])
        S.op("vector", lambda e: e.tensor_copy(rwT_s[:, :], rwstage[:, :]),
             reads=["rwst"], writes=["rwT"])

        def emit_ln(src_ap, src_keys, dst_ap, dst_keys, par=0):
            o = par * 8
            k = ["st%d_%d" % (i, par) for i in range(8)]
            S.op("vector", lambda e: e.reduce_sum(stat[:, o:o + 1], src_ap, axis=AX),
                 reads=src_keys, writes=[k[0]])
            S.op("vector", lambda e: e.tensor_scalar(stat[:, o + 1:o + 2],
                                                     stat[:, o:o + 1],
                                                     1.0 / D, None, ALU.mult),
                 reads=[k[0]], writes=[k[1]])
            S.op("scalar", lambda e: e.activation(dst_ap, src_ap, AF.Square,
                                                  accum_out=stat[:, o + 2:o + 3]),
                 reads=src_keys, writes=dst_keys + [k[2]])
            S.op("vector", lambda e: e.tensor_scalar(stat[:, o + 3:o + 4],
                                                     stat[:, o + 2:o + 3],
                                                     1.0 / D, LN_EPS, ALU.mult,
                                                     ALU.add),
                 reads=[k[2]], writes=[k[3]])
            S.op("vector", lambda e: e.tensor_tensor(stat[:, o + 4:o + 5],
                                                     stat[:, o + 1:o + 2],
                                                     stat[:, o + 1:o + 2], ALU.mult),
                 reads=[k[1]], writes=[k[4]])
            S.op("vector", lambda e: e.tensor_tensor(stat[:, o + 5:o + 6],
                                                     stat[:, o + 3:o + 4],
                                                     stat[:, o + 4:o + 5], ALU.subtract),
                 reads=[k[3], k[4]], writes=[k[5]])
            S.op("scalar", lambda e: e.activation(stat[:, o + 6:o + 7],
                                                  stat[:, o + 5:o + 6], AF.Sqrt),
                 reads=[k[5]], writes=[k[6]])
            S.op("vector", lambda e: e.reciprocal(stat[:, o + 7:o + 8],
                                                  stat[:, o + 6:o + 7]),
                 reads=[k[6]], writes=[k[7]])
            return S.op("vector", lambda e: e.tensor_scalar(
                dst_ap, src_ap, stat[:, o + 1:o + 2], stat[:, o + 7:o + 8],
                ALU.subtract, ALU.mult),
                reads=src_keys + [k[1], k[7]], writes=dst_keys)

        if "p1" in stages:
            S.op("vector", lambda e: e.memset(af("xh", 0, D), 0.0), writes=fkeys("xh", 0, D))
            S.dma("sync", lambda e: e.dma_start(out=af("xh", 0, D, 0, 3), in_=XH),
                  writes=fkeys("xh", 0, D))
            emit_ln(af("xh", 0, D), fkeys("xh", 0, D),
                    ab("evbA", 0, D), bkeys("evbA", 0, D))
            def tf_halo(e):
                last = None
                for j in range(ND):
                    last = e.transpose(pt[:, j * 128:(j + 1) * 128],
                                       ab("evbA", j * 128, 128), id_s[:, :])
                return last
            S.op("tensor", tf_halo, reads=bkeys("evbA", 0, D) + ["id"], writes=["pt"])
            def ev_halo(e):
                return e.tensor_copy(
                    aTh[:, :].rearrange("p (c m) -> p c m", m=3),
                    pt[:, 0:1024].rearrange("p (c m) -> p c m", m=128)[:, :, 0:3])
            S.op("vector", ev_halo, reads=["pt"], writes=["aTh"])

            for tt in range(NTT):
                par = "AB"[tt % 2]
                xa = af("xt" + par, 0, D)
                xk = fkeys("xt" + par, 0, D)
                S.dma("sync", lambda e, tt=tt, xa=xa: e.dma_start(
                    out=xa, in_=X[tt * 128:(tt + 1) * 128, :]), writes=xk)
                emit_ln(xa, xk, ab("evb" + par, 0, D), bkeys("evb" + par, 0, D),
                        par=tt % 2)
                def tf(e, par=par):
                    last = None
                    for j in range(ND):
                        last = e.transpose(pt[:, j * 128:(j + 1) * 128],
                                           ab("evb" + par, j * 128, 128), id_s[:, :])
                    return last
                S.op("tensor", tf, reads=bkeys("evb" + par, 0, D) + ["id"], writes=["pt"])
                def ev(e, tt=tt):
                    dst = ARB[:, AB["aT"]: AB["aT"] + ND * T] \
                        .rearrange("p (c m) -> p c m", m=T)[:, :, tt * 128:(tt + 1) * 128]
                    src = pt[:, 0:1024].rearrange("p (c m) -> p c m", m=128)
                    return e.tensor_copy(dst, src)
                S.op("vector", ev, reads=["pt"],
                     writes=bkeys("aT", 0, ND * T))



        if "p1" in stages:
            for f in range(NF):
                par = "AB"[f % 2]
                wstk = fkeys("wst" + par, 0, DI)
                wbfk = bkeys("wbf" + par, 0, DI)
                S.dma("sync", lambda e, f=f, par=par: e.dma_start(
                    out=af("wst" + par, 0, 1024).rearrange("p (c m) -> p c m", m=128),
                    in_=WINT[:, f * 128:(f + 1) * 128]
                        .rearrange("(c p) m -> p c m", p=128)),
                    writes=wstk)
                S.op("scalar", lambda e, par=par: e.copy(
                    ab("wbf" + par, 0, 1024), af("wst" + par, 0, 1024)),
                    reads=wstk, writes=wbfk)

                def mm(e, f=f, par=par):
                    last = None
                    for k in range(ND):
                        lhs = ab("wbf" + par, k * 128, 128)
                        for ts in range(NTS):
                            rhs = ARB[:, AB["aT"] + k * T + ts * TS:
                                      AB["aT"] + k * T + (ts + 1) * TS]
                            last = e.matmul(pbank[ts][:, :], lhs, rhs,
                                            start=(k == 0), stop=(k == ND - 1))
                    return last
                S.op("tensor", mm, reads=wbfk + bkeys("aT", 0, ND * T),
                     writes=["pb0", "pb1", "pb2", "pb3"])

                if f < NI:
                    xpk = bkeys("xpc" + par, 0, 3 + T)
                    for ts in range(NTS):
                        S.op("vector", lambda e, ts=ts, par=par: e.tensor_copy(
                            ab("xpc" + par, 3 + ts * TS, TS), pbank[ts][:, :]),
                            reads=[PB[ts]], writes=xpk)
                    def mmh(e, f=f, par=par):
                        last = None
                        for k in range(ND):
                            last = e.matmul(
                                pbank[4][:, 0:3], ab("wbf" + par, k * 128, 128),
                                aTh[:, k * 3:(k + 1) * 3],
                                start=(k == 0), stop=(k == ND - 1))
                        return last
                    S.op("tensor", mmh,
                         reads=bkeys("wbf" + par, 0, DI) + ["aTh"],
                         writes=["pb4"])
                    S.op("vector", lambda e, par=par: e.tensor_copy(
                        ab("xpc" + par, 0, 3), pbank[4][:, 0:3]),
                        reads=["pb4"], writes=xpk)
                    S.op("vector", lambda e, f=f, par=par: e.tensor_scalar(
                        ab("caccb", 0, T), ab("xpc" + par, 0, T),
                        convw_s[:, f * 4:f * 4 + 1], None, ALU.mult),
                        reads=xpk + ["convw"], writes=bkeys("caccb", 0, T))
                    for k in range(1, 4):
                        S.op("vector", lambda e, f=f, k=k, par=par:
                             e.scalar_tensor_tensor(
                                 ab("caccb", 0, T), ab("xpc" + par, k, T),
                                 convw_s[:, f * 4 + k:f * 4 + k + 1],
                                 ab("caccb", 0, T), ALU.mult, ALU.add),
                             reads=xpk + ["convw"] + bkeys("caccb", 0, T),
                             writes=bkeys("caccb", 0, T))
                    S.op("scalar", lambda e, f=f: e.activation(
                        ab("xc", f * T, T), ab("caccb", 0, T), AF.Silu,
                        bias=convb_s[:, f:f + 1]),
                        reads=bkeys("caccb", 0, T) + ["convb"],
                        writes=bkeys("xc", f * T, T))
                    g, rr = f // 4, (f % 4) * 128
                    for half in (0, 4):
                        S.dma("gpsimd", lambda e, g=g, rr=rr, half=half, f=f:
                              e.dma_start(
                                  out=contrib_fwd[(g + half) * FWD_ROWS + 512 + rr:
                                                  (g + half) * FWD_ROWS + 512 + rr + 128, :],
                                  in_=ab("xc", f * T, T)),
                              reads=bkeys("xc", f * T, T),
                              writes=["cfwd_u%d_%d" % (f, half)])
                else:
                    fr = f - NI
                    for ts in range(NTS):
                        S.op("vector", lambda e, ts=ts, par=par: e.tensor_copy(
                            ab("evb" + par, ts * TS, TS), pbank[ts][:, :]),
                            reads=[PB[ts]], writes=bkeys("evb" + par, 0, T))
                    S.dma("gpsimd", lambda e, fr=fr, par=par: e.dma_start(
                        out=res_d[fr * 128:(fr + 1) * 128, :], in_=ab("evb" + par, 0, T)),
                        reads=bkeys("evb" + par, 0, T), writes=["resd%d" % fr])

        if "p1" in stages:
            for f in range(NI):
                par = "AB"[f % 2]
                wstk = fkeys("wst" + par, 0, DI)
                wbfk = bkeys("wbf" + par, 0, DI)
                S.dma("sync", lambda e, f=f, par=par: e.dma_start(
                    out=af("wst" + par, 0, DI).rearrange("p (c m) -> p c m", m=128),
                    in_=WDTT[:, f * 128:(f + 1) * 128]
                        .rearrange("(c p) m -> p c m", p=128)),
                    writes=wstk)
                S.op("scalar", lambda e, par=par: e.copy(
                    ab("wbf" + par, 0, DI), af("wst" + par, 0, DI)),
                    reads=wstk, writes=wbfk)

                def mm(e, par=par):
                    last = None
                    for k in range(NI):
                        lhs = ab("wbf" + par, k * 128, 128)
                        for ts in range(NTS):
                            rhs = ARB[:, AB["xc"] + k * T + ts * TS:
                                      AB["xc"] + k * T + (ts + 1) * TS]
                            last = e.matmul(pbank[ts][:, :], lhs, rhs,
                                            start=(k == 0), stop=(k == NI - 1))
                    return last
                S.op("tensor", mm, reads=wbfk + bkeys("xc", 0, NI * T),
                     writes=["pb0", "pb1", "pb2", "pb3"])
                for ts in range(NTS):
                    S.op("vector", lambda e, ts=ts, f=f, par=par: e.tensor_scalar(
                        ab("evb" + par, ts * TS, TS), pbank[ts][:, :],
                        dtb_s[:, f:f + 1], None, ALU.add),
                        reads=[PB[ts], "dtb"], writes=bkeys("evb" + par, 0, T))
                g, rr = f // 4, (f % 4) * 128
                for half in (0, 4):
                    S.dma("gpsimd", lambda e, g=g, rr=rr, half=half, par=par:
                          e.dma_start(
                              out=contrib_fwd[(g + half) * FWD_ROWS + rr:
                                              (g + half) * FWD_ROWS + rr + 128, :],
                              in_=ab("evb" + par, 0, T)),
                          reads=bkeys("evb" + par, 0, T),
                          writes=["cfwd_dt%d_%d" % (f, half)])

        if "p1" in stages:
            S.dma("sync", lambda e: e.dma_start(
                out=af("wstA", 0, 512).rearrange("p (c m) -> p c m", m=32),
                in_=WXT.rearrange("(c p) m -> p c m", p=128)),
                writes=fkeys("wstA", 0, 512))
            S.op("scalar", lambda e: e.copy(ab("wbfA", 0, 512),
                                            af("wstA", 0, 512)),
                 reads=fkeys("wstA", 0, 512), writes=bkeys("wbfA", 0, 512))

            def mmx(e):
                last = None
                for k in range(NI):
                    lhs = ab("wbfA", k * 32, 32)
                    for ts in range(NTS):
                        rhs = ARB[:, AB["xc"] + k * T + ts * TS:
                                  AB["xc"] + k * T + (ts + 1) * TS]
                        last = e.matmul(pbank[ts][0:32, :], lhs, rhs,
                                        start=(k == 0), stop=(k == NI - 1))
                return last
            S.op("tensor", mmx, reads=bkeys("wbfA", 0, 512) + bkeys("xc", 0, NI * T),
                 writes=["pb0", "pb1", "pb2", "pb3"])
            for ts in range(NTS):
                S.op("vector", lambda e, ts=ts: e.tensor_copy(
                    ab("bcb", ts * TS, TS, 0, 32), pbank[ts][0:32, :]),
                    reads=[PB[ts]], writes=bkeys("bcb", 0, T))
            for gd in range(8):
                S.dma("gpsimd", lambda e, gd=gd: e.dma_start(
                    out=contrib_fwd[gd * FWD_ROWS + 1024: gd * FWD_ROWS + 1056, :]
                        .rearrange("(s two) t -> two s t", two=2),
                    in_=ab("bcb", 0, T, 0, 32)),
                    reads=bkeys("bcb", 0, T), writes=["cfwd_bc%d" % gd])

        if "p2" in stages:
            def cfw(e):
                with allow_ici_all_to_all():
                    return e.collective_compute(
                        "AllToAll", ALU.bypass, replica_groups=[list(range(8))],
                        ins=[contrib_fwd[:, :]], outs=[a2a_fwd[:, :]])
            cfwd_keys = (["cfwd_u%d_%d" % (f, h) for f in range(NI) for h in (0, 4)]
                         + ["cfwd_dt%d_%d" % (f, h) for f in range(NI) for h in (0, 4)]
                         + ["cfwd_bc%d" % gd for gd in range(8)])
            S.op("gpsimd", cfw, reads=cfwd_keys, writes=["a2af"], sem="cfw", amt=1)

        def carry_ap(j):
            return ARB[:, AB["carry"] + j: AB["carry"] + j + 1]

        if "p3" in stages:
            n_w_done = 0
            for w in range(NW):
                for ib in range(4):
                    S.dma("gpsimd", lambda e, w=w, ib=ib: e.indirect_dma_start(
                        out=ab("dtw", ib * T, T), out_offset=None,
                        in_=a2a_fwd[:, :],
                        in_offset=IndirectOffsetOnAxis(
                            gidx_s[:, w * 9 + ib:w * 9 + ib + 1], 0)),
                        reads=["a2af", "gidx"], writes=["dtw%d" % ib])
                    S.dma("gpsimd", lambda e, w=w, ib=ib: e.indirect_dma_start(
                        out=ab("uw", ib * T, T), out_offset=None,
                        in_=a2a_fwd[:, :],
                        in_offset=IndirectOffsetOnAxis(
                            gidx_s[:, w * 9 + 4 + ib:w * 9 + 4 + ib + 1], 0)),
                        reads=["a2af", "gidx"], writes=["uw%d" % ib])
                for ib in range(4):
                    S.op("vector", lambda e, ib=ib: e.tensor_mul(
                        ab("dtuw", ib * T, T), ab("dtw", ib * T, T),
                        ab("uw", ib * T, T)),
                        reads=["dtw%d" % ib, "uw%d" % ib],
                        writes=["dtu%d" % ib])

                def emit_repl(w, pr):
                    par2 = pr % 2
                    gc = 52 + w * 8 + pr
                    S.dma("gpsimd", lambda e, gc=gc, par2=par2: e.indirect_dma_start(
                        out=ab("bc2", par2 * 8192, 8192), out_offset=None,
                        in_=a2a_fwd[:, :],
                        in_offset=IndirectOffsetOnAxis(gidx_s[:, gc:gc + 1], 0)),
                        reads=["a2af", "gidx"], writes=["bc2_%d" % par2])

                for ib in range(4):
                    emit_repl(w, 0)
                    emit_repl(w, 1)
                    for s in range(16):
                        if s % 2 == 0 and s // 2 + 2 < 8:
                            emit_repl(w, s // 2 + 2)
                        par2 = (s // 2) % 2
                        boff = par2 * 8192 + (s % 2) * 4096
                        sp = s % 2
                        hp = s % 2
                        dqi = sp * T
                        dk = ["dax%d" % sp]
                        bk2 = ["btx%d" % sp]
                        hk = ["h2_%d" % hp]
                        S.op("scalar", lambda e, s=s, ib=ib, dqi=dqi:
                             e.activation(ab("dax", dqi, T),
                                          ab("dtw", ib * T, T),
                                          AF.Exp, scale=alog_s[:, s:s + 1]),
                             reads=["dtw%d" % ib, "alog"], writes=dk)
                        S.op("vector", lambda e, ib=ib, dqi=dqi, boff=boff:
                             e.tensor_mul(ab("btx", dqi, T),
                                          ab("dtuw", ib * T, T),
                                          ab("bc2", boff, T)),
                             reads=["dtu%d" % ib, "bc2_%d" % par2],
                             writes=bk2)
                        j = s * 4 + ib
                        init = 0.0 if n_w_done == 0 else carry_ap(j)
                        S.op("vector", lambda e, dqi=dqi, hp=hp, init=init:
                             e.tensor_tensor_scan(
                                 out=ab("h2", hp * T, T),
                                 data0=ab("dax", dqi, T),
                                 data1=ab("btx", dqi, T),
                                 initial=init, op0=ALU.mult, op1=ALU.add),
                             reads=(dk + bk2
                                    + ([] if n_w_done == 0 else bkeys("carry", 0, 64))),
                             writes=hk)
                        S.op("vector", lambda e, j=j, hp=hp: e.tensor_copy(
                            ARB[:, AB["carry"] + j: AB["carry"] + j + 1],
                            ab("h2", hp * T + T - 1, 1)),
                            reads=hk, writes=bkeys("carry", 0, 64))
                        S.op("vector", lambda e, hp=hp, boff=boff:
                             e.tensor_mul(ab("hc2", hp * T, T),
                                          ab("h2", hp * T, T),
                                          ab("bc2", boff + 2048, T)),
                             reads=hk + ["bc2_%d" % par2],
                             writes=["hc2_%d" % hp])

                        def seg(e, s=s, hp=hp):
                            last = None
                            for cc in range(4):
                                last = e.matmul(
                                    pbank[cc][:, :], id_s[:, :],
                                    ab("hc2", hp * T + cc * CC, CC),
                                    start=(s == 0), stop=(s == 15))
                            return last
                        S.op("tensor", seg,
                             reads=["hc2_%d" % hp, "id"],
                             writes=["pb0", "pb1", "pb2", "pb3"])
                    for cc in range(4):
                        S.op("vector", lambda e, ib=ib, cc=cc:
                             e.scalar_tensor_tensor(
                                 ab("dtuw", ib * T + cc * CC, CC),
                                 ab("uw", ib * T + cc * CC, CC),
                                 dvec_s[:, ib:ib + 1], pbank[cc][:, :],
                                 ALU.mult, ALU.add),
                             reads=["uw%d" % ib, "dvec", PB[cc]],
                             writes=["dtu%d" % ib])
                n_w_done += 1
                for ib in range(4):
                    for half in (0, 4):
                        S.dma("gpsimd", lambda e, w=w, ib=ib, half=half: e.dma_start(
                            out=contrib_y[(w + half) * CH + ib * 128:
                                          (w + half) * CH + (ib + 1) * 128, :],
                            in_=ab("dtuw", ib * T, T)),
                            reads=["dtu%d" % ib],
                            writes=["cy_%d_%d_%d" % (w, ib, half)])

        if "p4" in stages:
            def cy(e):
                with allow_ici_all_to_all():
                    return e.collective_compute(
                        "AllToAll", ALU.bypass, replica_groups=[list(range(8))],
                        ins=[contrib_y[:, :]], outs=[a2a_y[:, :]])
            cy_keys = ["cy_%d_%d_%d" % (w, ib, h) for w in range(NW)
                       for ib in range(4) for h in (0, 4)]
            S.op("gpsimd", cy, reads=cy_keys, writes=["a2ay"], sem="cy", amt=1)

        if "p5" in stages:
            for f in range(NI):
                gs = f // 4
                isub = f % 4
                S.dma("gpsimd", lambda e, gs=gs, isub=isub: e.indirect_dma_start(
                    out=ab("ytile", 0, T), out_offset=None, in_=a2a_y[:, :],
                    in_offset=IndirectOffsetOnAxis(
                        gidx_s[:, 36 + gs * 4 + isub:37 + gs * 4 + isub], 0)),
                    reads=["a2ay", "gidx"], writes=bkeys("ytile", 0, T))
                S.dma("sync", lambda e, f=f: e.dma_start(
                    out=ab("rstile", 0, T), in_=res_d[f * 128:(f + 1) * 128, :]),
                    reads=["resd%d" % f], writes=bkeys("rstile", 0, T))
                S.op("scalar", lambda e: e.activation(
                    ab("silur", 0, T), ab("rstile", 0, T), AF.Silu),
                    reads=bkeys("rstile", 0, T), writes=bkeys("silur", 0, T))
                S.op("vector", lambda e, f=f: e.tensor_mul(
                    ab("gated", f * T, T), ab("ytile", 0, T), ab("silur", 0, T)),
                    reads=bkeys("ytile", 0, T) + bkeys("silur", 0, T),
                    writes=bkeys("gated", f * T, T))

            for ts in range(NTS):
                for m in range(ND):
                    par = "CD"[m % 2]
                    S.dma("sync", lambda e, m=m, par=par: e.dma_start(
                        out=af("wst" + ("A" if par == "C" else "B"), 0, DI)
                            .rearrange("p (c m) -> p c m", m=128),
                        in_=WOUTT[:, m * 128:(m + 1) * 128]
                            .rearrange("(c p) m -> p c m", p=128)),
                        writes=fkeys("wst" + ("A" if par == "C" else "B"), 0, DI))
                    S.op("scalar", lambda e, par=par: e.copy(
                        ab("wbf" + par, 0, DI),
                        af("wst" + ("A" if par == "C" else "B"), 0, DI)),
                        reads=fkeys("wst" + ("A" if par == "C" else "B"), 0, DI),
                        writes=bkeys("wbf" + par, 0, DI))

                    def mmo(e, ts=ts, m=m, par=par):
                        last = None
                        for k in range(NI):
                            last = e.matmul(
                                pbank[m % 2][:, :], ab("wbf" + par, k * 128, 128),
                                ARB[:, AB["gated"] + k * T + ts * TS:
                                    AB["gated"] + k * T + (ts + 1) * TS],
                                start=(k == 0), stop=(k == NI - 1))
                        return last
                    S.op("tensor", mmo,
                         reads=bkeys("wbf" + par, 0, DI) + bkeys("gated", 0, NI * T),
                         writes=[PB[m % 2]])
                    S.op("vector", lambda e, m=m: e.tensor_copy(
                        ab("outcm", m * TS, TS), pbank[m % 2][:, :]),
                        reads=[PB[m % 2]], writes=bkeys("outcm", m * TS, TS))

                for tsub in range(4):
                    tt = ts * 4 + tsub
                    parx = "AB"[tt % 2]
                    xa = af("x0" + parx, 0, D)
                    xk = fkeys("x0" + parx, 0, D)
                    S.dma("sync", lambda e, tt=tt, xa=xa: e.dma_start(
                        out=xa, in_=X[tt * 128:(tt + 1) * 128, :]), writes=xk)

                    def tro(e, tsub=tsub):
                        last = None
                        for m in range(ND):
                            last = e.transpose(
                                pt[:, m * 128:(m + 1) * 128],
                                ARB[:, AB["outcm"] + m * TS + tsub * 128:
                                    AB["outcm"] + m * TS + (tsub + 1) * 128],
                                id_s[:, :])
                        return last
                    S.op("tensor", tro, reads=bkeys("outcm", 0, ND * TS) + ["id"],
                         writes=["pt"])
                    emit_ln(xa, xk, ab("ytile", 0, D), bkeys("ytile", 0, D))
                    S.op("vector", lambda e: e.tensor_add(
                        af("zt", 0, D), pt[:, 0:D], ab("ytile", 0, D)),
                        reads=["pt"] + bkeys("ytile", 0, D), writes=fkeys("zt", 0, D))
                    emit_ln(af("zt", 0, D), fkeys("zt", 0, D),
                            af("mt", 0, D), fkeys("mt", 0, D))
                    S.op("vector", lambda e: e.tensor_mul(
                        af("at2", 0, D), af("mt", 0, D), gb_s[:, 0:D]),
                        reads=fkeys("mt", 0, D) + ["gb"], writes=fkeys("at2", 0, D))
                    S.op("vector", lambda e: e.tensor_add(
                        af("at2", 0, D), af("at2", 0, D), gb_s[:, D:2 * D]),
                        reads=fkeys("at2", 0, D) + ["gb"], writes=fkeys("at2", 0, D))
                    S.op("vector", lambda e, xa=xa: e.tensor_add(
                        af("x1t", 0, D), af("at2", 0, D), xa),
                        reads=fkeys("at2", 0, D) + xk, writes=fkeys("x1t", 0, D))
                    S.dma("gpsimd", lambda e, tt=tt: e.dma_start(
                        out=x1_d[tt * 128:(tt + 1) * 128, :], in_=af("x1t", 0, D)),
                        reads=fkeys("x1t", 0, D), writes=["x1d%d" % tt])
                    emit_ln(af("x1t", 0, D), fkeys("x1t", 0, D),
                            ab("ytile", 0, D), bkeys("ytile", 0, D))

                    def trh(e):
                        last = None
                        for j in range(ND):
                            last = e.transpose(pt[:, j * 128:(j + 1) * 128],
                                               ab("ytile", j * 128, 128), id_s[:, :])
                        return last
                    S.op("tensor", trh, reads=bkeys("ytile", 0, D) + ["id"],
                         writes=["pt"])

                    def evh(e, tt=tt):
                        dst = ARB[:, AB["hT"]: AB["hT"] + ND * T] \
                            .rearrange("p (c m) -> p c m", m=T)[:, :, tt * 128:(tt + 1) * 128]
                        src = pt[:, 0:1024].rearrange("p (c m) -> p c m", m=128)
                        return e.tensor_copy(dst, src)
                    S.op("vector", evh, reads=["pt"], writes=bkeys("hT", 0, ND * T))

                    def mmr(e, tt=tt):
                        last = None
                        for k in range(ND):
                            last = e.matmul(
                                pbank[6][:, 0:7],
                                ARB[:, AB["hT"] + k * T + tt * 128:
                                    AB["hT"] + k * T + (tt + 1) * 128],
                                rwT_s[:, k * 7:(k + 1) * 7],
                                start=(k == 0), stop=(k == ND - 1))
                        return last
                    S.op("tensor", mmr, reads=bkeys("hT", 0, ND * T) + ["rwT"],
                         writes=["pb6"])
                    S.op("vector", lambda e: e.tensor_copy(lg_t[:, 0:7], pbank[6][:, 0:7]),
                         reads=["pb6"], writes=["lg"])
                    S.op("vector", lambda e: e.tensor_add(lg_t[:, 0:7], lg_t[:, 0:7],
                                                          rbias_s[:, 0:7]),
                         reads=["lg", "rbias"], writes=["lg"])
                    S.op("vector", lambda e: e.tensor_scalar(
                        lg_t[:, 0:7], lg_t[:, 0:7], -50.0, 50.0, ALU.max, ALU.min),
                        reads=["lg"], writes=["lg"])
                    S.op("scalar", lambda e: e.activation(prb[:, 0:7], lg_t[:, 0:7],
                                                          AF.Sigmoid),
                         reads=["lg"], writes=["prb"])
                    S.op("vector", lambda e: e.reduce_max(pmax[:, :], prb[:, 0:7], axis=AX),
                         reads=["prb"], writes=["pmax"])
                    S.op("vector", lambda e: e.tensor_scalar(
                        pm7[:, 0:7], ones7[:, 0:7], pmax[:, 0:1], None, ALU.mult),
                        reads=["ones7", "pmax"], writes=["pm7"])
                    S.op("vector", lambda e: e.tensor_tensor(
                        equ[:, 0:7], prb[:, 0:7], pm7[:, 0:7], ALU.is_equal),
                        reads=["prb", "pm7"], writes=["equ"])
                    S.op("vector", lambda e: e.select(
                        eff[:, 0:7], equ[:, 0:7], ones7[:, 0:7], zero7[:, 0:7]),
                        reads=["equ", "ones7", "zero7"], writes=["eff"])
                    S.op("vector", lambda e: e.tensor_tensor_scan(
                        out=pre[:, 0:7], data0=ones7[:, 0:7], data1=eff[:, 0:7],
                        initial=0.0, op0=ALU.mult, op1=ALU.add),
                        reads=["ones7", "eff"], writes=["pre"])
                    S.op("vector", lambda e: e.tensor_tensor(
                        msk[:, 0:7], pre[:, 0:7], eff[:, 0:7], ALU.subtract),
                        reads=["pre", "eff"], writes=["msk"])
                    S.op("vector", lambda e: e.tensor_tensor(
                        msk[:, 0:7], eff[:, 0:7], msk[:, 0:7], ALU.subtract),
                        reads=["eff", "msk"], writes=["msk"])
                    S.op("vector", lambda e: e.tensor_scalar(
                        msk[:, 0:7], msk[:, 0:7], 0.0, None, ALU.max),
                        reads=["msk"], writes=["msk"])
                    S.op("vector", lambda e: e.tensor_scalar(
                        pclp[:, 0:1], pmax[:, 0:1], 1e-8, 1.0 - 1e-8, ALU.max, ALU.min),
                        reads=["pmax"], writes=["pclp"])
                    S.op("vector", lambda e: e.tensor_scalar(
                        pclp[:, 1:2], pclp[:, 0:1], 0.5, 1e-8, ALU.add, ALU.add),
                        reads=["pclp"], writes=["pclp"])
                    S.op("vector", lambda e: e.tensor_scalar(
                        pclp[:, 1:2], pclp[:, 1:2], 0.5, 2.0, ALU.max, ALU.min),
                        reads=["pclp"], writes=["pclp"])
                    S.op("vector", lambda e: e.reciprocal(pclp[:, 2:3], pclp[:, 1:2]),
                         reads=["pclp"], writes=["pclp"])
                    S.op("vector", lambda e, tt=tt: e.tensor_scalar(
                        w8cat[:, tt * 8:tt * 8 + 7], msk[:, 0:7], pclp[:, 0:1],
                        pclp[:, 2:3], ALU.mult, ALU.mult),
                        reads=["msk", "pclp"], writes=["w8c"])
                    S.op("vector", lambda e, tt=tt: e.tensor_scalar(
                        w8cat[:, tt * 8 + 7:tt * 8 + 8], pclp[:, 2:3], 0.5, None,
                        ALU.mult),
                        reads=["pclp"], writes=["w8c"])

            S.op("vector", lambda e: e.tensor_copy(w8b[:, :], w8cat[:, :]),
                 reads=["w8c"], writes=["w8b"])
            S.op("tensor", lambda e: e.transpose(pt[:, 0:128], w8b[:, :], id_s[:, :]),
                 reads=["w8b", "id"], writes=["pt"])
            S.op("vector", lambda e: e.tensor_copy(w8b[:, :], pt[:, 0:128]),
                 reads=["pt"], writes=["w8b"])
            S.dma("gpsimd", lambda e: e.dma_start(
                out=w8r_d.rearrange("(e8 tt) t -> tt e8 t", tt=16),
                in_=w8b[:, :]),
                reads=["w8b"], writes=["w8d"])

        if "p5" in stages:
            for tsp in range(2):
                tok0 = tsp * 1024
                for e_i in range(E + 1):
                    S.dma("gpsimd", lambda e, e_i=e_i, tsp=tsp: e.indirect_dma_start(
                        out=ab("wbc", 0, 1024), out_offset=None,
                        in_=w8r_d[:, :],
                        in_offset=IndirectOffsetOnAxis(
                            gidx_s[:, 84 + e_i * 2 + tsp:85 + e_i * 2 + tsp], 0)),
                        reads=["w8d", "gidx"], writes=bkeys("wbc", 0, 1024))

                    for mb in range(11):
                        par = "AB"[mb % 2]
                        wstk = fkeys("wstA", 0, 4096)
                        w13k = bkeys("w13" + par, 0, 4096)
                        for hh in range(2):
                            S.dma("sync", lambda e, e_i=e_i, mb=mb, hh=hh: e.dma_start(
                                out=af("wstA", hh * 2048, 2048)
                                    .rearrange("p (c m) -> p c m", m=512),
                                in_=W13R[e_i, mb, hh * 512:(hh + 1) * 512, :]
                                    .rearrange("(c p) m -> p c m", p=128)),
                                writes=wstk)
                        S.op("scalar", lambda e, par=par: e.copy(
                            ab("w13" + par, 0, 4096), af("wstA", 0, 4096)),
                            reads=wstk, writes=w13k)

                        for mi in range(2):
                            m = mb * 2 + mi
                            def mg(e, par=par, mi=mi, tok0=tok0, woff=0, p0=0):
                                last = None
                                for k in range(ND):
                                    lhs = ARB[:, AB["w13" + par] + k * 512 + woff + mi * 128:
                                              AB["w13" + par] + k * 512 + woff + (mi + 1) * 128]
                                    for t2 in range(2):
                                        rhs = ARB[:, AB["hT"] + k * T + tok0 + t2 * TS:
                                                  AB["hT"] + k * T + tok0 + (t2 + 1) * TS]
                                        last = e.matmul(pbank[p0 + t2][:, :], lhs, rhs,
                                                        start=(k == 0), stop=(k == ND - 1))
                                return last
                            S.op("tensor", lambda e, par=par, mi=mi, tok0=tok0:
                                 mg(e, par, mi, tok0, 0, 0),
                                 reads=w13k + bkeys("hT", 0, ND * T),
                                 writes=["pb0", "pb1"])
                            for t2 in range(2):
                                S.op("scalar", lambda e, m=m, t2=t2: e.activation(
                                    ab("gg", m * 1024 + t2 * TS, TS),
                                    pbank[t2][:, :], AF.Silu),
                                    reads=[PB[t2]],
                                    writes=bkeys("gg", m * 1024, 1024))
                            S.op("tensor", lambda e, par=par, mi=mi, tok0=tok0:
                                 mg(e, par, mi, tok0, 256, 2),
                                 reads=w13k + bkeys("hT", 0, ND * T),
                                 writes=["pb2", "pb3"])
                            so_ = 8192 + (mi % 2) * 1024
                            for t2 in range(2):
                                S.op("vector", lambda e, t2=t2, so_=so_:
                                     e.scalar_tensor_tensor(
                                         ab("w2h", so_ + t2 * TS, TS),
                                         pbank[2 + t2][:, :], 1.0,
                                         ab("wbc", t2 * TS, TS),
                                         ALU.mult, ALU.mult),
                                     reads=[PB[2 + t2]] + bkeys("wbc", 0, 1024),
                                     writes=bkeys("w2h", so_, 1024))
                            for t2 in range(2):
                                S.op("gpsimd", lambda e, m=m, t2=t2, so_=so_:
                                     e.tensor_mul(
                                         ab("gg", m * 1024 + t2 * TS, TS),
                                         ab("gg", m * 1024 + t2 * TS, TS),
                                         ab("w2h", so_ + t2 * TS, TS)),
                                     reads=bkeys("w2h", so_, 1024)
                                           + bkeys("gg", m * 1024, 1024),
                                     writes=bkeys("gg", m * 1024, 1024))

                    for kh in range(2):
                        kn = 11
                        for kk in range(kn):
                            k = kh * 11 + kk
                            par = "AB"[kk % 2]
                            S.dma("sync", lambda e, e_i=e_i, k=k, par=par: e.dma_start(
                                out=af("wst" + par, 0, D),
                                in_=W2T[e_i, k * 128:(k + 1) * 128, :]),
                                writes=fkeys("wst" + par, 0, D))
                            S.op("vector", lambda e, k=k, kk=kk, par=par: e.tensor_copy(
                                ab("w2h", kk * 1024, 1024), af("wst" + par, 0, D)),
                                reads=fkeys("wst" + par, 0, D),
                                writes=bkeys("w2h", kk * 1024, 1024))
                        for tt2 in range(8):
                            def mw2(e, kh=kh, tt2=tt2, tok0=tok0):
                                last = None
                                for kk in range(11):
                                    k = kh * 11 + kk
                                    lhs = ARB[:, AB["gg"] + k * 1024 + tt2 * 128:
                                              AB["gg"] + k * 1024 + (tt2 + 1) * 128]
                                    for n in range(2):
                                        last = e.matmul(
                                            pbank[4 + n][:, :],
                                            lhs,
                                            ARB[:, AB["w2h"] + kk * 1024 + n * 512:
                                                AB["w2h"] + kk * 1024 + (n + 1) * 512],
                                            start=(kk == 0), stop=(kk == 10))
                                return last
                            S.op("tensor", mw2,
                                 reads=bkeys("gg", 0, NII * 1024) + bkeys("w2h", 0, 11 * 1024),
                                 writes=["pb4", "pb5"])
                            tt = tsp * 8 + tt2
                            for n in range(2):
                                if e_i == 0 and kh == 0:
                                    S.op("vector", lambda e, tt=tt, n=n: e.tensor_copy(
                                        ab("acc", tt * 1024 + n * 512, 512),
                                        pbank[4 + n][:, :]),
                                        reads=[PB[4 + n]],
                                        writes=bkeys("acc", tt * 1024, 1024))
                                else:
                                    S.op("vector", lambda e, tt=tt, n=n: e.tensor_add(
                                        ab("acc", tt * 1024 + n * 512, 512),
                                        ab("acc", tt * 1024 + n * 512, 512),
                                        pbank[4 + n][:, :]),
                                        reads=[PB[4 + n]] + bkeys("acc", tt * 1024, 1024),
                                        writes=bkeys("acc", tt * 1024, 1024))

            for tt in range(NTT):
                parx = "AB"[tt % 2]
                xa = af("x0" + parx, 0, D)
                xk = fkeys("x0" + parx, 0, D)
                S.dma("sync", lambda e, tt=tt, xa=xa: e.dma_start(
                    out=xa, in_=x1_d[tt * 128:(tt + 1) * 128, :]),
                    reads=["x1d%d" % tt], writes=xk)
                S.op("vector", lambda e, tt=tt, xa=xa: e.tensor_add(
                    af("zt", 0, D), ab("acc", tt * 1024, 1024), xa),
                    reads=bkeys("acc", tt * 1024, 1024) + xk,
                    writes=fkeys("zt", 0, D))
                S.dma("gpsimd", lambda e, tt=tt: e.dma_start(
                    out=OUT[tt * 128:(tt + 1) * 128, :], in_=af("zt", 0, D)),
                    reads=fkeys("zt", 0, D), writes=["outd"])

        if "res" in debug:
            DR = dbg("res", (DI, T), BF16)
            S.dma("gpsimd", lambda e: e.dma_start(out=DR, in_=res_d[:, :]),
                  reads=["resd%d" % i for i in range(NI)], writes=["dbgr"])
        if "cfwd" in debug:
            DF = dbg("cfwd", (8 * FWD_ROWS, T), BF16)
            S.dma("gpsimd", lambda e: e.dma_start(out=DF, in_=contrib_fwd[:, :]),
                  reads=(["cfwd_u%d_%d" % (f, h) for f in range(NI) for h in (0, 4)]
                         + ["cfwd_dt%d_%d" % (f, h) for f in range(NI) for h in (0, 4)]
                         + ["cfwd_bc%d" % gd for gd in range(8)]), writes=["dbgf"])
        if "a2af" in debug:
            DA2 = dbg("a2af", (8 * FWD_ROWS, T), BF16)
            S.dma("gpsimd", lambda e: e.dma_start(out=DA2, in_=a2a_fwd[:, :]),
                  reads=["a2af"], writes=["dbga"])
        if "ydbg" in debug:
            DY = dbg("ydbg", (8 * CH, T), BF16)
            S.dma("gpsimd", lambda e: e.dma_start(out=DY, in_=contrib_y[:, :]),
                  reads=["cy_%d_%d_%d" % (w, ib, h) for w in range(NW)
                         for ib in range(4) for h in (0, 4)], writes=["dbgy"])
        if "a2ay" in debug:
            DY2 = dbg("a2ay", (8 * CH, T), BF16)
            S.dma("gpsimd", lambda e: e.dma_start(out=DY2, in_=a2a_y[:, :]),
                  reads=["a2ay"], writes=["dbgy2"])
        if "x1" in debug:
            DX1 = dbg("x1", (T, D))
            S.dma("gpsimd", lambda e: e.dma_start(out=DX1, in_=x1_d[:, :]),
                  reads=["x1d%d" % tt for tt in range(NTT)], writes=["dbgx1"])

        final = dict(S.cnt)
        S.op("sync", lambda e: e.nop(),
             deps=[("ld", final.get("ld", 0))])
        S.op("gpsimd", lambda e: e.nop(),
             deps=[("gq", final.get("gq", 0)), ("aq", final.get("aq", 0))])

        @block.sync
        def _(sync):
            S.emit_engine("sync", sync, sems)

        @block.tensor
        def _(tensor):
            S.emit_engine("tensor", tensor, sems)

        @block.vector
        def _(vector):
            S.emit_engine("vector", vector, sems)

        @block.scalar
        def _(scalar):
            S.emit_engine("scalar", scalar, sems)

        @block.gpsimd
        def _(gpsimd):
            S.emit_engine("gpsimd", gpsimd, sems)

    return nc, dbg_outs


def kernel(**inputs):
    shared, per_core = _host_prep(inputs)
    nc, _ = build_program()
    in_maps = []
    for c in range(8):
        m = dict(shared)
        m.update(per_core[c])
        in_maps.append(m)
    res = run_bass_kernel_spmd(nc, in_maps, core_ids=list(range(8)))
    out = np.empty((B, L, D), np.float32)
    for c in range(8):
        b, g = c // 4, c % 4
        out[b, g * T:(g + 1) * T] = np.asarray(res.results[c]["out"], np.float32)
    return out


# version 13
# speedup vs baseline: 1.1349x; 1.0064x over previous
import numpy as np
import ml_dtypes
from contextlib import ExitStack

import concourse.bass as bass
import concourse.mybir as mybir
from concourse.bass_utils import run_bass_kernel_spmd
from concourse.bass_core_rust import IndirectOffsetOnAxis
from concourse.aot_env import allow_ici_all_to_all

F32 = mybir.dt.float32
BF16 = mybir.dt.bfloat16
U32 = mybir.dt.uint32
AF = mybir.ActivationFunctionType
ALU = mybir.AluOpType
AX = mybir.AxisListType.X

B, L, D = 2, 8192, 1024
T = 2048
DI, DS, II, E = 2048, 16, 2816, 7
ND = D // 128
NF = 4096 // 128
NI = DI // 128
NII = II // 128
NTT = T // 128
TS = 512
NTS = T // TS
CH = DI // 4
NW = 4
CC = 512
FWD_ROWS = 1056
LN_EPS = 1e-5
NGC = 100
BS = 2048

ARB_N = 67584
ARF_N = 14336
AB = {
    "xc": 0, "aT": 32768, "wbfA": 49152, "wbfB": 51200,
    "xpcA": 53248, "xpcB": 55304, "evbA": 57360, "evbB": 59408,
    "bcb": 61456, "caccb": 63504,
    "dtw": 0, "uw": 8192, "dtuw": 16384, "h2": 24576,
    "dax": 28672, "btx": 32768, "bc2": 36864,
    "hc2": 53248, "carry": 57344,
    "hT": 0, "gated": 16384, "outcm": 49152, "ytile": 53248,
    "rstile": 55296, "silur": 57344, "wbfC": 59392, "wbfD": 61440,
    "gg": 16384, "w2h": 38912, "acc": 50176, "wbc": 66560,
    "w13A": 38912, "w13B": 43008,
}
AFm = {
    "xtA": 0, "xtB": 1024, "xh": 2048, "scr1": 3072, "x0A": 4096,
    "x0B": 5120, "x1t": 6144, "zt": 7168, "mt": 8192, "at2": 9216,
    "wstA": 10240, "wstB": 12288,
}


def _host_prep(inputs):
    f32 = np.float32
    asc = np.ascontiguousarray
    x = np.asarray(inputs["L0_attn_x"], f32)
    win = np.asarray(inputs["L0_attn_in_proj_w"], f32)
    convw = np.asarray(inputs["L0_attn_conv_w"], f32)[:, 0, :]
    convb = np.asarray(inputs["L0_attn_conv_b"], f32)
    wx = np.asarray(inputs["L0_attn_x_proj_w"], f32)
    wdt = np.asarray(inputs["L0_attn_dt_proj_w"], f32)
    dtb = np.asarray(inputs["L0_attn_dt_proj_b"], f32)
    alog = np.asarray(inputs["L0_attn_A_log"], f32)
    dvec = np.asarray(inputs["L0_attn_D"], f32)
    wout = np.asarray(inputs["L0_attn_out_proj_w"], f32)
    gamma = np.asarray(inputs["L0_attn_ln_gamma"], f32)
    beta = np.asarray(inputs["L0_attn_ln_beta"], f32)
    rw = np.asarray(inputs["L0_ffn_router_w"], f32)
    rb = np.asarray(inputs["L0_ffn_routing_bias"], f32)
    sw1 = np.asarray(inputs["L0_ffn_sw1"], f32)
    sw2 = np.asarray(inputs["L0_ffn_sw2"], f32)
    sw3 = np.asarray(inputs["L0_ffn_sw3"], f32)
    ew1 = np.asarray(inputs["L0_ffn_ew1"], f32)
    ew2 = np.asarray(inputs["L0_ffn_ew2"], f32)
    ew3 = np.asarray(inputs["L0_ffn_ew3"], f32)

    oh = np.zeros((32, 32, 128), dtype=ml_dtypes.bfloat16)
    for j in range(32):
        oh[:, j, :][j, :] = 1.0
    shared = {
        "winT": asc(win.T),
        "wxT": asc(wx.T),
        "wdtT": asc(wdt.T),
        "woutT": asc(wout.T),
        "convw_r": asc(convw.reshape(NI, 128, 4).transpose(1, 0, 2).reshape(128, NI * 4)),
        "convb_r": asc(convb.reshape(NI, 128).T),
        "dtb_r": asc(dtb.reshape(NI, 128).T),
        "alog_r": asc(np.tile(alog, (128, 1))),
        "gamma_r": asc(gamma.reshape(1, D)),
        "beta_r": asc(beta.reshape(1, D)),
        "rwT": asc(rw.T),
        "rbias_r": asc(np.tile(rb, (128, 1))),
        "w1T": asc(np.concatenate([ew1.transpose(0, 2, 1), sw1.T[None]], 0)),
        "w3T": asc(np.concatenate([ew3.transpose(0, 2, 1), sw3.T[None]], 0)),
        "w2T": asc(np.concatenate([ew2.transpose(0, 2, 1), sw2.T[None]], 0)),
        "ident_b": np.eye(128, dtype=ml_dtypes.bfloat16),
        "ones_b": np.ones((1, 128), dtype=ml_dtypes.bfloat16),
        "ones_f": np.ones((1, 128), dtype=f32),
        "onehot": asc(oh.reshape(32, 32 * 128)),
    }

    w1b = shared["w1T"].reshape(8, 1024, 11, 256)
    w3b = shared["w3T"].reshape(8, 1024, 11, 256)
    shared["w13R"] = asc(np.concatenate([w1b, w3b], axis=-1).transpose(0, 2, 1, 3))
    del shared["w1T"], shared["w3T"]

    per_core = []
    for c in range(8):
        b, g = c // 4, c % 4
        xs = asc(x[b, g * T:(g + 1) * T])
        halo = np.zeros((3, D), f32) if g == 0 else asc(x[b, g * T - 3:g * T])
        dvec_my = asc(dvec.reshape(NI, 128).T[:, g * 4:(g + 1) * 4])
        gidx = np.zeros((128, NGC), np.uint32)
        base = 0 if c < 4 else 4 * FWD_ROWS
        ybase = 0 if c < 4 else 4 * CH
        r = np.arange(128, dtype=np.uint32)
        for w in range(NW):
            for ib in range(4):
                gidx[:, w * 9 + ib] = base + w * FWD_ROWS + ib * 128 + r
                gidx[:, w * 9 + 4 + ib] = base + w * FWD_ROWS + 512 + ib * 128 + r
            gidx[0:32, w * 9 + 8] = base + w * FWD_ROWS + 1024 + np.arange(32, dtype=np.uint32)
        for gs in range(4):
            for isub in range(4):
                gidx[:, 36 + gs * 4 + isub] = ybase + gs * CH + isub * 128 + r
        for w in range(NW):
            for sp2 in range(8):
                gidx[:, 52 + w * 8 + sp2] = base + w * FWD_ROWS + 1024 + 4 * sp2
        for e_i in range(8):
            for tsp in range(2):
                gidx[:, 84 + e_i * 2 + tsp] = e_i * 16 + tsp * 8
        per_core.append({"x": xs, "xhalo": halo, "dvec_my": dvec_my, "gidx": gidx})
    return shared, per_core


class Sched:
    ENG_SEM = {"tensor": "tv", "vector": "vv", "scalar": "sv",
               "gpsimd": "pv", "sync": "yv"}

    def __init__(self):
        self.ops = []
        self.cnt = {}
        self.last_w = {}
        self.readers = {}

    def _tok(self, sem, amt):
        self.cnt[sem] = self.cnt.get(sem, 0) + amt
        return (sem, self.cnt[sem])

    def op(self, eng, fn, reads=(), writes=(), sem=None, amt=1, deps=()):
        waits = {}

        def need(tok):
            if tok is None:
                return
            s, v = tok
            if waits.get(s, 0) < v:
                waits[s] = v
        for bn in reads:
            need(self.last_w.get(bn))
        for bn in writes:
            need(self.last_w.get(bn))
            for t in self.readers.get(bn, ()):
                need(t)
        for t in deps:
            need(t)
        if sem is None:
            sem = self.ENG_SEM[eng]
        tok = self._tok(sem, amt)
        self.ops.append((eng, waits, fn, sem, amt))
        for bn in writes:
            self.last_w[bn] = tok
            self.readers[bn] = []
        for bn in reads:
            self.readers.setdefault(bn, []).append(tok)
        return tok

    def dma(self, queue, fn, reads=(), writes=(), deps=()):
        qsem = {"sync": "ld", "gpsimd": "gq", "scalar": "aq"}[queue]
        return self.op(queue, fn, reads=reads, writes=writes, sem=qsem,
                       amt=16, deps=deps)

    def emit_engine(self, eng_name, eng, sems):
        wm = {}
        for en, waits, fn, sem, amt in self.ops:
            if en != eng_name:
                continue
            for s, v in sorted(waits.items()):
                if wm.get(s, 0) < v:
                    eng.wait_ge(sems[s], v)
                    wm[s] = v
            inst = fn(eng)
            assert inst is not None
            inst.then_inc(sems[sem], amt)


def bkeys(name, off, n):
    c0 = AB[name] + off
    return ["ab%d" % i for i in range(c0 // BS, -(-(c0 + n) // BS))]


def fkeys(name, off, n):
    c0 = AFm[name] + off
    return ["af%d" % i for i in range(c0 // BS, -(-(c0 + n) // BS))]


def build_program(stages=("p1", "p2", "p3", "p4", "p5"), debug=()):
    nc = bass.Bass("TRN2", target_bir_lowering=False, debug=False)
    S = Sched()

    def dram_in(name, shape, dt=F32):
        return nc.dram_tensor(name, list(shape), dt, kind="ExternalInput").ap()

    X = dram_in("x", (T, D))
    XH = dram_in("xhalo", (3, D))
    WINT = dram_in("winT", (D, 4096))
    WXT = dram_in("wxT", (DI, 32))
    WDTT = dram_in("wdtT", (DI, DI))
    WOUTT = dram_in("woutT", (DI, D))
    CONVW = dram_in("convw_r", (128, NI * 4))
    CONVB = dram_in("convb_r", (128, NI))
    DTB = dram_in("dtb_r", (128, NI))
    ALOG = dram_in("alog_r", (128, NI))
    DVECM = dram_in("dvec_my", (128, 4))
    GAMMA = dram_in("gamma_r", (1, D))
    BETA = dram_in("beta_r", (1, D))
    RWT = dram_in("rwT", (D, 7))
    RBIAS = dram_in("rbias_r", (128, 7))
    W13R = dram_in("w13R", (E + 1, 11, D, 512))
    W2T = dram_in("w2T", (E + 1, II, D))
    IDENT = dram_in("ident_b", (128, 128), BF16)
    ONESB = dram_in("ones_b", (1, 128), BF16)
    ONESF = dram_in("ones_f", (1, 128))
    OH = dram_in("onehot", (32, 32 * 128), BF16)
    GIDX = dram_in("gidx", (128, NGC), U32)

    OUT = nc.dram_tensor("out", [T, D], F32, kind="ExternalOutput").ap()

    contrib_fwd = nc.dram_tensor("contrib_fwd", [8 * FWD_ROWS, T], BF16).ap()
    a2a_fwd = nc.dram_tensor("a2a_fwd", [8 * FWD_ROWS, T], BF16).ap()
    contrib_y = nc.dram_tensor("contrib_y", [8 * CH, T], BF16).ap()
    a2a_y = nc.dram_tensor("a2a_y", [8 * CH, T], BF16).ap()
    res_d = nc.dram_tensor("res_d", [DI, T], BF16).ap()
    x1_d = nc.dram_tensor("x1_d", [T, D], F32).ap()
    w8_d = nc.dram_tensor("w8_d", [128, 128], BF16).ap()
    w8r_d = nc.dram_tensor("w8r_d", [128, 128], BF16).ap()

    dbg_outs = {}

    def dbg(name, shape, dt=F32):
        ap = nc.dram_tensor("dbg_" + name, list(shape), dt, kind="ExternalOutput").ap()
        dbg_outs[name] = ap
        return ap

    with ExitStack() as ctx:
        def sb(name, shape, dt=F32):
            return ctx.enter_context(nc.sbuf_tensor(name, list(shape), dt))

        def psum(name, shape, dt=F32):
            return ctx.enter_context(nc.psum_tensor(name, list(shape), dt))

        ARB = sb("ARB", (128, ARB_N), BF16)
        ARF = sb("ARF", (128, ARF_N))

        def ab(name, off=0, n=None, r0=0, r1=128):
            c0 = AB[name] + off
            return ARB[r0:r1, c0:c0 + n]

        def af(name, off=0, n=None, r0=0, r1=128):
            c0 = AFm[name] + off
            return ARF[r0:r1, c0:c0 + n]

        ARFB = ARF[:, 0:5120].bitcast(BF16)

        def afb(off, n):
            return ARFB[:, off:off + n]

        gidx_s = sb("gidx_s", (128, NGC), U32)
        alog_s = sb("alog_s", (128, NI))
        convw_s = sb("convw_s", (128, NI * 4))
        convb_s = sb("convb_s", (128, NI))
        dtb_s = sb("dtb_s", (128, NI))
        dvec_s = sb("dvec_s", (128, 4))
        rbias_s = sb("rbias_s", (128, 7))
        oh_s = sb("oh_s", (32, 32 * 128), BF16)
        id_s = sb("id_s", (128, 128), BF16)
        onesb_s = sb("onesb_s", (1, 128), BF16)
        onesf_s = sb("onesf_s", (1, 128))
        gb_s = sb("gb_s", (128, 2 * D))
        ones7 = sb("ones7", (128, 8))
        zero7 = sb("zero7", (128, 8))
        rwT_s = sb("rwT_s", (128, ND * 7), BF16)
        aTh = sb("aTh", (128, 24), BF16)
        rwstage = sb("rwstage", (128, ND * 7))
        stat = sb("stat", (128, 16))
        lg_t = sb("lg_t", (128, 8))
        prb = sb("prb", (128, 8))
        pmax = sb("pmax", (128, 1))
        pm7 = sb("pm7", (128, 8))
        equ = sb("equ", (128, 8), U32)
        eff = sb("eff", (128, 8))
        pre = sb("pre", (128, 8))
        msk = sb("msk", (128, 8))
        pclp = sb("pclp", (128, 4))
        w8cat = sb("w8cat", (128, 128))
        w8b = sb("w8b", (128, 128), BF16)

        def wrow_ap(c0, n):
            return ARB[0:1, AB["wbc"] + c0: AB["wbc"] + c0 + n]

        pbank = [psum("pb%d" % i, (128, 512)) for i in range(7)]
        pt = psum("pt", (128, 1024), BF16)
        PB = ["pb%d" % i for i in range(7)]

        sem_names = ["ld", "gq", "aq", "tv", "vv", "sv", "pv", "yv", "cfw", "cy"]
        sems = {n: ctx.enter_context(nc.semaphore(n)) for n in sem_names}
        block = ctx.enter_context(nc.Block())

        for ap_, t_, nm in [(GIDX, gidx_s, "gidx"), (ALOG, alog_s, "alog"),
                            (CONVW, convw_s, "convw"), (CONVB, convb_s, "convb"),
                            (DTB, dtb_s, "dtb"), (DVECM, dvec_s, "dvec"),
                            (RBIAS, rbias_s, "rbias"), (OH, oh_s, "oh"),
                            (IDENT, id_s, "id"), (ONESB, onesb_s, "onesb"),
                            (ONESF, onesf_s, "onesf")]:
            S.dma("sync", lambda e, ap_=ap_, t_=t_: e.dma_start(out=t_[:, :], in_=ap_),
                  writes=[nm])
        S.op("vector", lambda e: e.memset(ones7[:, :], 1.0), writes=["ones7"])
        S.op("vector", lambda e: e.memset(zero7[:, :], 0.0), writes=["zero7"])
        S.dma("sync", lambda e: e.dma_start(out=af("wstA", 0, D, 0, 1), in_=GAMMA),
              writes=fkeys("wstA", 0, 2 * D))
        S.dma("sync", lambda e: e.dma_start(out=af("wstA", D, D, 0, 1), in_=BETA),
              writes=fkeys("wstA", 0, 2 * D))
        for j in range(4):
            S.op("tensor",
                 lambda e, j=j: e.matmul(pbank[4][:, :], onesf_s[0:1, :],
                                         af("wstA", j * 512, 512, 0, 1),
                                         start=True, stop=True),
                 reads=fkeys("wstA", 0, 2 * D) + ["onesf"], writes=["pb4"])
            S.op("vector",
                 lambda e, j=j: e.tensor_copy(gb_s[:, j * 512:(j + 1) * 512],
                                              pbank[4][:, :]),
                 reads=["pb4"], writes=["gb"])
        S.dma("sync", lambda e: e.dma_start(
            out=rwstage[:, :].rearrange("p (c m) -> p c m", m=7),
            in_=RWT.rearrange("(c p) m -> p c m", p=128)),
            writes=["rwst"])
        S.op("vector", lambda e: e.tensor_copy(rwT_s[:, :], rwstage[:, :]),
             reads=["rwst"], writes=["rwT"])

        def emit_ln(src_ap, src_keys, dst_ap, dst_keys, par=0):
            o = par * 8
            k = ["st%d_%d" % (i, par) for i in range(8)]
            S.op("vector", lambda e: e.reduce_sum(stat[:, o:o + 1], src_ap, axis=AX),
                 reads=src_keys, writes=[k[0]])
            S.op("vector", lambda e: e.tensor_scalar(stat[:, o + 1:o + 2],
                                                     stat[:, o:o + 1],
                                                     1.0 / D, None, ALU.mult),
                 reads=[k[0]], writes=[k[1]])
            S.op("scalar", lambda e: e.activation(dst_ap, src_ap, AF.Square,
                                                  accum_out=stat[:, o + 2:o + 3]),
                 reads=src_keys, writes=dst_keys + [k[2]])
            S.op("vector", lambda e: e.tensor_scalar(stat[:, o + 3:o + 4],
                                                     stat[:, o + 2:o + 3],
                                                     1.0 / D, LN_EPS, ALU.mult,
                                                     ALU.add),
                 reads=[k[2]], writes=[k[3]])
            S.op("vector", lambda e: e.tensor_tensor(stat[:, o + 4:o + 5],
                                                     stat[:, o + 1:o + 2],
                                                     stat[:, o + 1:o + 2], ALU.mult),
                 reads=[k[1]], writes=[k[4]])
            S.op("vector", lambda e: e.tensor_tensor(stat[:, o + 5:o + 6],
                                                     stat[:, o + 3:o + 4],
                                                     stat[:, o + 4:o + 5], ALU.subtract),
                 reads=[k[3], k[4]], writes=[k[5]])
            S.op("scalar", lambda e: e.activation(stat[:, o + 6:o + 7],
                                                  stat[:, o + 5:o + 6], AF.Sqrt),
                 reads=[k[5]], writes=[k[6]])
            S.op("vector", lambda e: e.reciprocal(stat[:, o + 7:o + 8],
                                                  stat[:, o + 6:o + 7]),
                 reads=[k[6]], writes=[k[7]])
            return S.op("vector", lambda e: e.tensor_scalar(
                dst_ap, src_ap, stat[:, o + 1:o + 2], stat[:, o + 7:o + 8],
                ALU.subtract, ALU.mult),
                reads=src_keys + [k[1], k[7]], writes=dst_keys)

        if "p1" in stages:
            S.op("vector", lambda e: e.memset(af("xh", 0, D), 0.0), writes=fkeys("xh", 0, D))
            S.dma("sync", lambda e: e.dma_start(out=af("xh", 0, D, 0, 3), in_=XH),
                  writes=fkeys("xh", 0, D))
            emit_ln(af("xh", 0, D), fkeys("xh", 0, D),
                    ab("evbA", 0, D), bkeys("evbA", 0, D))
            def tf_halo(e):
                last = None
                for j in range(ND):
                    last = e.transpose(pt[:, j * 128:(j + 1) * 128],
                                       ab("evbA", j * 128, 128), id_s[:, :])
                return last
            S.op("tensor", tf_halo, reads=bkeys("evbA", 0, D) + ["id"], writes=["pt"])
            def ev_halo(e):
                return e.tensor_copy(
                    aTh[:, :].rearrange("p (c m) -> p c m", m=3),
                    pt[:, 0:1024].rearrange("p (c m) -> p c m", m=128)[:, :, 0:3])
            S.op("vector", ev_halo, reads=["pt"], writes=["aTh"])

            for tt in range(NTT):
                par = "AB"[tt % 2]
                xa = af("xt" + par, 0, D)
                xk = fkeys("xt" + par, 0, D)
                S.dma("sync", lambda e, tt=tt, xa=xa: e.dma_start(
                    out=xa, in_=X[tt * 128:(tt + 1) * 128, :]), writes=xk)
                emit_ln(xa, xk, ab("evb" + par, 0, D), bkeys("evb" + par, 0, D),
                        par=tt % 2)
                def tf(e, par=par):
                    last = None
                    for j in range(ND):
                        last = e.transpose(pt[:, j * 128:(j + 1) * 128],
                                           ab("evb" + par, j * 128, 128), id_s[:, :])
                    return last
                S.op("tensor", tf, reads=bkeys("evb" + par, 0, D) + ["id"], writes=["pt"])
                def ev(e, tt=tt):
                    dst = ARB[:, AB["aT"]: AB["aT"] + ND * T] \
                        .rearrange("p (c m) -> p c m", m=T)[:, :, tt * 128:(tt + 1) * 128]
                    src = pt[:, 0:1024].rearrange("p (c m) -> p c m", m=128)
                    return e.tensor_copy(dst, src)
                S.op("vector", ev, reads=["pt"],
                     writes=bkeys("aT", 0, ND * T))



        if "p1" in stages:
            for f in range(NF):
                par = "AB"[f % 2]
                wstk = fkeys("wst" + par, 0, DI)
                wbfk = bkeys("wbf" + par, 0, DI)
                S.dma("sync", lambda e, f=f, par=par: e.dma_start(
                    out=af("wst" + par, 0, 1024).rearrange("p (c m) -> p c m", m=128),
                    in_=WINT[:, f * 128:(f + 1) * 128]
                        .rearrange("(c p) m -> p c m", p=128)),
                    writes=wstk)
                S.op("scalar", lambda e, par=par: e.copy(
                    ab("wbf" + par, 0, 1024), af("wst" + par, 0, 1024)),
                    reads=wstk, writes=wbfk)

                def mm(e, f=f, par=par):
                    last = None
                    for k in range(ND):
                        lhs = ab("wbf" + par, k * 128, 128)
                        for ts in range(NTS):
                            rhs = ARB[:, AB["aT"] + k * T + ts * TS:
                                      AB["aT"] + k * T + (ts + 1) * TS]
                            last = e.matmul(pbank[ts][:, :], lhs, rhs,
                                            start=(k == 0), stop=(k == ND - 1))
                    return last
                S.op("tensor", mm, reads=wbfk + bkeys("aT", 0, ND * T),
                     writes=["pb0", "pb1", "pb2", "pb3"])

                if f < NI:
                    xpk = bkeys("xpc" + par, 0, 3 + T)
                    for ts in range(NTS):
                        S.op("vector", lambda e, ts=ts, par=par: e.tensor_copy(
                            ab("xpc" + par, 3 + ts * TS, TS), pbank[ts][:, :]),
                            reads=[PB[ts]], writes=xpk)
                    def mmh(e, f=f, par=par):
                        last = None
                        for k in range(ND):
                            last = e.matmul(
                                pbank[4][:, 0:3], ab("wbf" + par, k * 128, 128),
                                aTh[:, k * 3:(k + 1) * 3],
                                start=(k == 0), stop=(k == ND - 1))
                        return last
                    S.op("tensor", mmh,
                         reads=bkeys("wbf" + par, 0, DI) + ["aTh"],
                         writes=["pb4"])
                    S.op("vector", lambda e, par=par: e.tensor_copy(
                        ab("xpc" + par, 0, 3), pbank[4][:, 0:3]),
                        reads=["pb4"], writes=xpk)
                    S.op("vector", lambda e, f=f, par=par: e.tensor_scalar(
                        ab("caccb", 0, T), ab("xpc" + par, 0, T),
                        convw_s[:, f * 4:f * 4 + 1], None, ALU.mult),
                        reads=xpk + ["convw"], writes=bkeys("caccb", 0, T))
                    for k in range(1, 4):
                        S.op("vector", lambda e, f=f, k=k, par=par:
                             e.scalar_tensor_tensor(
                                 ab("caccb", 0, T), ab("xpc" + par, k, T),
                                 convw_s[:, f * 4 + k:f * 4 + k + 1],
                                 ab("caccb", 0, T), ALU.mult, ALU.add),
                             reads=xpk + ["convw"] + bkeys("caccb", 0, T),
                             writes=bkeys("caccb", 0, T))
                    S.op("scalar", lambda e, f=f: e.activation(
                        ab("xc", f * T, T), ab("caccb", 0, T), AF.Silu,
                        bias=convb_s[:, f:f + 1]),
                        reads=bkeys("caccb", 0, T) + ["convb"],
                        writes=bkeys("xc", f * T, T))
                    g, rr = f // 4, (f % 4) * 128
                    for half in (0, 4):
                        S.dma("gpsimd", lambda e, g=g, rr=rr, half=half, f=f:
                              e.dma_start(
                                  out=contrib_fwd[(g + half) * FWD_ROWS + 512 + rr:
                                                  (g + half) * FWD_ROWS + 512 + rr + 128, :],
                                  in_=ab("xc", f * T, T)),
                              reads=bkeys("xc", f * T, T),
                              writes=["cfwd_u%d_%d" % (f, half)])
                else:
                    fr = f - NI
                    for ts in range(NTS):
                        S.op("vector", lambda e, ts=ts, par=par: e.tensor_copy(
                            ab("evb" + par, ts * TS, TS), pbank[ts][:, :]),
                            reads=[PB[ts]], writes=bkeys("evb" + par, 0, T))
                    S.dma("gpsimd", lambda e, fr=fr, par=par: e.dma_start(
                        out=res_d[fr * 128:(fr + 1) * 128, :], in_=ab("evb" + par, 0, T)),
                        reads=bkeys("evb" + par, 0, T), writes=["resd%d" % fr])

        if "p1" in stages:
            for f in range(NI):
                par = "AB"[f % 2]
                wstk = fkeys("wst" + par, 0, DI)
                wbfk = bkeys("wbf" + par, 0, DI)
                S.dma("sync", lambda e, f=f, par=par: e.dma_start(
                    out=af("wst" + par, 0, DI).rearrange("p (c m) -> p c m", m=128),
                    in_=WDTT[:, f * 128:(f + 1) * 128]
                        .rearrange("(c p) m -> p c m", p=128)),
                    writes=wstk)
                S.op("scalar", lambda e, par=par: e.copy(
                    ab("wbf" + par, 0, DI), af("wst" + par, 0, DI)),
                    reads=wstk, writes=wbfk)

                def mm(e, par=par):
                    last = None
                    for k in range(NI):
                        lhs = ab("wbf" + par, k * 128, 128)
                        for ts in range(NTS):
                            rhs = ARB[:, AB["xc"] + k * T + ts * TS:
                                      AB["xc"] + k * T + (ts + 1) * TS]
                            last = e.matmul(pbank[ts][:, :], lhs, rhs,
                                            start=(k == 0), stop=(k == NI - 1))
                    return last
                S.op("tensor", mm, reads=wbfk + bkeys("xc", 0, NI * T),
                     writes=["pb0", "pb1", "pb2", "pb3"])
                for ts in range(NTS):
                    S.op("vector", lambda e, ts=ts, f=f, par=par: e.tensor_scalar(
                        ab("evb" + par, ts * TS, TS), pbank[ts][:, :],
                        dtb_s[:, f:f + 1], None, ALU.add),
                        reads=[PB[ts], "dtb"], writes=bkeys("evb" + par, 0, T))
                g, rr = f // 4, (f % 4) * 128
                for half in (0, 4):
                    S.dma("gpsimd", lambda e, g=g, rr=rr, half=half, par=par:
                          e.dma_start(
                              out=contrib_fwd[(g + half) * FWD_ROWS + rr:
                                              (g + half) * FWD_ROWS + rr + 128, :],
                              in_=ab("evb" + par, 0, T)),
                          reads=bkeys("evb" + par, 0, T),
                          writes=["cfwd_dt%d_%d" % (f, half)])

        if "p1" in stages:
            S.dma("sync", lambda e: e.dma_start(
                out=af("wstA", 0, 512).rearrange("p (c m) -> p c m", m=32),
                in_=WXT.rearrange("(c p) m -> p c m", p=128)),
                writes=fkeys("wstA", 0, 512))
            S.op("scalar", lambda e: e.copy(ab("wbfA", 0, 512),
                                            af("wstA", 0, 512)),
                 reads=fkeys("wstA", 0, 512), writes=bkeys("wbfA", 0, 512))

            def mmx(e):
                last = None
                for k in range(NI):
                    lhs = ab("wbfA", k * 32, 32)
                    for ts in range(NTS):
                        rhs = ARB[:, AB["xc"] + k * T + ts * TS:
                                  AB["xc"] + k * T + (ts + 1) * TS]
                        last = e.matmul(pbank[ts][0:32, :], lhs, rhs,
                                        start=(k == 0), stop=(k == NI - 1))
                return last
            S.op("tensor", mmx, reads=bkeys("wbfA", 0, 512) + bkeys("xc", 0, NI * T),
                 writes=["pb0", "pb1", "pb2", "pb3"])
            for ts in range(NTS):
                S.op("vector", lambda e, ts=ts: e.tensor_copy(
                    ab("bcb", ts * TS, TS, 0, 32), pbank[ts][0:32, :]),
                    reads=[PB[ts]], writes=bkeys("bcb", 0, T))
            for gd in range(8):
                S.dma("gpsimd", lambda e, gd=gd: e.dma_start(
                    out=contrib_fwd[gd * FWD_ROWS + 1024: gd * FWD_ROWS + 1056, :]
                        .rearrange("(s two) t -> two s t", two=2),
                    in_=ab("bcb", 0, T, 0, 32)),
                    reads=bkeys("bcb", 0, T), writes=["cfwd_bc%d" % gd])

        if "p2" in stages:
            def cfw(e):
                with allow_ici_all_to_all():
                    return e.collective_compute(
                        "AllToAll", ALU.bypass, replica_groups=[list(range(8))],
                        ins=[contrib_fwd[:, :]], outs=[a2a_fwd[:, :]])
            cfwd_keys = (["cfwd_u%d_%d" % (f, h) for f in range(NI) for h in (0, 4)]
                         + ["cfwd_dt%d_%d" % (f, h) for f in range(NI) for h in (0, 4)]
                         + ["cfwd_bc%d" % gd for gd in range(8)])
            S.op("gpsimd", cfw, reads=cfwd_keys, writes=["a2af"], sem="cfw", amt=1)

        def carry_ap(j):
            return ARB[:, AB["carry"] + j: AB["carry"] + j + 1]

        if "p3" in stages:
            n_w_done = 0
            for w in range(NW):
                for ib in range(4):
                    S.dma("gpsimd", lambda e, w=w, ib=ib: e.indirect_dma_start(
                        out=ab("dtw", ib * T, T), out_offset=None,
                        in_=a2a_fwd[:, :],
                        in_offset=IndirectOffsetOnAxis(
                            gidx_s[:, w * 9 + ib:w * 9 + ib + 1], 0)),
                        reads=["a2af", "gidx"], writes=["dtw%d" % ib])
                    S.dma("gpsimd", lambda e, w=w, ib=ib: e.indirect_dma_start(
                        out=ab("uw", ib * T, T), out_offset=None,
                        in_=a2a_fwd[:, :],
                        in_offset=IndirectOffsetOnAxis(
                            gidx_s[:, w * 9 + 4 + ib:w * 9 + 4 + ib + 1], 0)),
                        reads=["a2af", "gidx"], writes=["uw%d" % ib])
                for ib in range(4):
                    S.op("vector", lambda e, ib=ib: e.tensor_mul(
                        ab("dtuw", ib * T, T), ab("dtw", ib * T, T),
                        ab("uw", ib * T, T)),
                        reads=["dtw%d" % ib, "uw%d" % ib],
                        writes=["dtu%d" % ib])

                def emit_repl(w, pr):
                    par2 = pr % 2
                    gc = 52 + w * 8 + pr
                    S.dma("gpsimd", lambda e, gc=gc, par2=par2: e.indirect_dma_start(
                        out=ab("bc2", par2 * 8192, 8192), out_offset=None,
                        in_=a2a_fwd[:, :],
                        in_offset=IndirectOffsetOnAxis(gidx_s[:, gc:gc + 1], 0)),
                        reads=["a2af", "gidx"], writes=["bc2_%d" % par2])

                for ib in range(4):
                    emit_repl(w, 0)
                    emit_repl(w, 1)
                    for s in range(16):
                        if s % 2 == 0 and s // 2 + 2 < 8:
                            emit_repl(w, s // 2 + 2)
                        par2 = (s // 2) % 2
                        boff = par2 * 8192 + (s % 2) * 4096
                        sp = s % 2
                        hp = s % 2
                        dqi = sp * T
                        dk = ["dax%d" % sp]
                        bk2 = ["btx%d" % sp]
                        hk = ["h2_%d" % hp]
                        S.op("scalar", lambda e, s=s, ib=ib, dqi=dqi:
                             e.activation(ab("dax", dqi, T),
                                          ab("dtw", ib * T, T),
                                          AF.Exp, scale=alog_s[:, s:s + 1]),
                             reads=["dtw%d" % ib, "alog"], writes=dk)
                        S.op("vector", lambda e, ib=ib, dqi=dqi, boff=boff:
                             e.tensor_mul(ab("btx", dqi, T),
                                          ab("dtuw", ib * T, T),
                                          ab("bc2", boff, T)),
                             reads=["dtu%d" % ib, "bc2_%d" % par2],
                             writes=bk2)
                        j = s * 4 + ib
                        init = 0.0 if n_w_done == 0 else carry_ap(j)
                        S.op("vector", lambda e, dqi=dqi, hp=hp, init=init:
                             e.tensor_tensor_scan(
                                 out=ab("h2", hp * T, T),
                                 data0=ab("dax", dqi, T),
                                 data1=ab("btx", dqi, T),
                                 initial=init, op0=ALU.mult, op1=ALU.add),
                             reads=(dk + bk2
                                    + ([] if n_w_done == 0 else bkeys("carry", 0, 64))),
                             writes=hk)
                        S.op("vector", lambda e, j=j, hp=hp: e.tensor_copy(
                            ARB[:, AB["carry"] + j: AB["carry"] + j + 1],
                            ab("h2", hp * T + T - 1, 1)),
                            reads=hk, writes=bkeys("carry", 0, 64))
                        S.op("vector", lambda e, hp=hp, boff=boff:
                             e.tensor_mul(ab("hc2", hp * T, T),
                                          ab("h2", hp * T, T),
                                          ab("bc2", boff + 2048, T)),
                             reads=hk + ["bc2_%d" % par2],
                             writes=["hc2_%d" % hp])

                        def seg(e, s=s, hp=hp):
                            last = None
                            for cc in range(4):
                                last = e.matmul(
                                    pbank[cc][:, :], id_s[:, :],
                                    ab("hc2", hp * T + cc * CC, CC),
                                    start=(s == 0), stop=(s == 15))
                            return last
                        S.op("tensor", seg,
                             reads=["hc2_%d" % hp, "id"],
                             writes=["pb0", "pb1", "pb2", "pb3"])
                    for cc in range(4):
                        S.op("vector", lambda e, ib=ib, cc=cc:
                             e.scalar_tensor_tensor(
                                 ab("dtuw", ib * T + cc * CC, CC),
                                 ab("uw", ib * T + cc * CC, CC),
                                 dvec_s[:, ib:ib + 1], pbank[cc][:, :],
                                 ALU.mult, ALU.add),
                             reads=["uw%d" % ib, "dvec", PB[cc]],
                             writes=["dtu%d" % ib])
                n_w_done += 1
                for ib in range(4):
                    for half in (0, 4):
                        S.dma("gpsimd", lambda e, w=w, ib=ib, half=half: e.dma_start(
                            out=contrib_y[(w + half) * CH + ib * 128:
                                          (w + half) * CH + (ib + 1) * 128, :],
                            in_=ab("dtuw", ib * T, T)),
                            reads=["dtu%d" % ib],
                            writes=["cy_%d_%d_%d" % (w, ib, half)])

        if "p4" in stages:
            def cy(e):
                with allow_ici_all_to_all():
                    return e.collective_compute(
                        "AllToAll", ALU.bypass, replica_groups=[list(range(8))],
                        ins=[contrib_y[:, :]], outs=[a2a_y[:, :]])
            cy_keys = ["cy_%d_%d_%d" % (w, ib, h) for w in range(NW)
                       for ib in range(4) for h in (0, 4)]
            S.op("gpsimd", cy, reads=cy_keys, writes=["a2ay"], sem="cy", amt=1)

        if "p5" in stages:
            for f in range(NI):
                gs = f // 4
                isub = f % 4
                S.dma("gpsimd", lambda e, gs=gs, isub=isub: e.indirect_dma_start(
                    out=ab("ytile", 0, T), out_offset=None, in_=a2a_y[:, :],
                    in_offset=IndirectOffsetOnAxis(
                        gidx_s[:, 36 + gs * 4 + isub:37 + gs * 4 + isub], 0)),
                    reads=["a2ay", "gidx"], writes=bkeys("ytile", 0, T))
                S.dma("sync", lambda e, f=f: e.dma_start(
                    out=ab("rstile", 0, T), in_=res_d[f * 128:(f + 1) * 128, :]),
                    reads=["resd%d" % f], writes=bkeys("rstile", 0, T))
                S.op("scalar", lambda e: e.activation(
                    ab("silur", 0, T), ab("rstile", 0, T), AF.Silu),
                    reads=bkeys("rstile", 0, T), writes=bkeys("silur", 0, T))
                S.op("vector", lambda e, f=f: e.tensor_mul(
                    ab("gated", f * T, T), ab("ytile", 0, T), ab("silur", 0, T)),
                    reads=bkeys("ytile", 0, T) + bkeys("silur", 0, T),
                    writes=bkeys("gated", f * T, T))

            for ts in range(NTS):
                for m in range(ND):
                    par = "CD"[m % 2]
                    S.dma("sync", lambda e, m=m, par=par: e.dma_start(
                        out=af("wst" + ("A" if par == "C" else "B"), 0, DI)
                            .rearrange("p (c m) -> p c m", m=128),
                        in_=WOUTT[:, m * 128:(m + 1) * 128]
                            .rearrange("(c p) m -> p c m", p=128)),
                        writes=fkeys("wst" + ("A" if par == "C" else "B"), 0, DI))
                    S.op("scalar", lambda e, par=par: e.copy(
                        ab("wbf" + par, 0, DI),
                        af("wst" + ("A" if par == "C" else "B"), 0, DI)),
                        reads=fkeys("wst" + ("A" if par == "C" else "B"), 0, DI),
                        writes=bkeys("wbf" + par, 0, DI))

                    def mmo(e, ts=ts, m=m, par=par):
                        last = None
                        for k in range(NI):
                            last = e.matmul(
                                pbank[m % 2][:, :], ab("wbf" + par, k * 128, 128),
                                ARB[:, AB["gated"] + k * T + ts * TS:
                                    AB["gated"] + k * T + (ts + 1) * TS],
                                start=(k == 0), stop=(k == NI - 1))
                        return last
                    S.op("tensor", mmo,
                         reads=bkeys("wbf" + par, 0, DI) + bkeys("gated", 0, NI * T),
                         writes=[PB[m % 2]])
                    S.op("vector", lambda e, m=m: e.tensor_copy(
                        ab("outcm", m * TS, TS), pbank[m % 2][:, :]),
                        reads=[PB[m % 2]], writes=bkeys("outcm", m * TS, TS))

                for tsub in range(4):
                    tt = ts * 4 + tsub
                    parx = "AB"[tt % 2]
                    xa = af("x0" + parx, 0, D)
                    xk = fkeys("x0" + parx, 0, D)
                    S.dma("sync", lambda e, tt=tt, xa=xa: e.dma_start(
                        out=xa, in_=X[tt * 128:(tt + 1) * 128, :]), writes=xk)

                    def tro(e, tsub=tsub):
                        last = None
                        for m in range(ND):
                            last = e.transpose(
                                pt[:, m * 128:(m + 1) * 128],
                                ARB[:, AB["outcm"] + m * TS + tsub * 128:
                                    AB["outcm"] + m * TS + (tsub + 1) * 128],
                                id_s[:, :])
                        return last
                    S.op("tensor", tro, reads=bkeys("outcm", 0, ND * TS) + ["id"],
                         writes=["pt"])
                    emit_ln(xa, xk, ab("ytile", 0, D), bkeys("ytile", 0, D))
                    S.op("vector", lambda e: e.tensor_add(
                        af("zt", 0, D), pt[:, 0:D], ab("ytile", 0, D)),
                        reads=["pt"] + bkeys("ytile", 0, D), writes=fkeys("zt", 0, D))
                    emit_ln(af("zt", 0, D), fkeys("zt", 0, D),
                            af("mt", 0, D), fkeys("mt", 0, D))
                    S.op("vector", lambda e: e.tensor_mul(
                        af("at2", 0, D), af("mt", 0, D), gb_s[:, 0:D]),
                        reads=fkeys("mt", 0, D) + ["gb"], writes=fkeys("at2", 0, D))
                    S.op("vector", lambda e: e.tensor_add(
                        af("at2", 0, D), af("at2", 0, D), gb_s[:, D:2 * D]),
                        reads=fkeys("at2", 0, D) + ["gb"], writes=fkeys("at2", 0, D))
                    S.op("vector", lambda e, xa=xa: e.tensor_add(
                        af("x1t", 0, D), af("at2", 0, D), xa),
                        reads=fkeys("at2", 0, D) + xk, writes=fkeys("x1t", 0, D))
                    S.dma("gpsimd", lambda e, tt=tt: e.dma_start(
                        out=x1_d[tt * 128:(tt + 1) * 128, :], in_=af("x1t", 0, D)),
                        reads=fkeys("x1t", 0, D), writes=["x1d%d" % tt])
                    emit_ln(af("x1t", 0, D), fkeys("x1t", 0, D),
                            ab("ytile", 0, D), bkeys("ytile", 0, D))

                    def trh(e):
                        last = None
                        for j in range(ND):
                            last = e.transpose(pt[:, j * 128:(j + 1) * 128],
                                               ab("ytile", j * 128, 128), id_s[:, :])
                        return last
                    S.op("tensor", trh, reads=bkeys("ytile", 0, D) + ["id"],
                         writes=["pt"])

                    def evh(e, tt=tt):
                        dst = ARB[:, AB["hT"]: AB["hT"] + ND * T] \
                            .rearrange("p (c m) -> p c m", m=T)[:, :, tt * 128:(tt + 1) * 128]
                        src = pt[:, 0:1024].rearrange("p (c m) -> p c m", m=128)
                        return e.tensor_copy(dst, src)
                    S.op("vector", evh, reads=["pt"], writes=bkeys("hT", 0, ND * T))

                    def mmr(e, tt=tt):
                        last = None
                        for k in range(ND):
                            last = e.matmul(
                                pbank[6][:, 0:7],
                                ARB[:, AB["hT"] + k * T + tt * 128:
                                    AB["hT"] + k * T + (tt + 1) * 128],
                                rwT_s[:, k * 7:(k + 1) * 7],
                                start=(k == 0), stop=(k == ND - 1))
                        return last
                    S.op("tensor", mmr, reads=bkeys("hT", 0, ND * T) + ["rwT"],
                         writes=["pb6"])
                    S.op("vector", lambda e: e.tensor_copy(lg_t[:, 0:7], pbank[6][:, 0:7]),
                         reads=["pb6"], writes=["lg"])
                    S.op("vector", lambda e: e.tensor_add(lg_t[:, 0:7], lg_t[:, 0:7],
                                                          rbias_s[:, 0:7]),
                         reads=["lg", "rbias"], writes=["lg"])
                    S.op("vector", lambda e: e.tensor_scalar(
                        lg_t[:, 0:7], lg_t[:, 0:7], -50.0, 50.0, ALU.max, ALU.min),
                        reads=["lg"], writes=["lg"])
                    S.op("scalar", lambda e: e.activation(prb[:, 0:7], lg_t[:, 0:7],
                                                          AF.Sigmoid),
                         reads=["lg"], writes=["prb"])
                    S.op("vector", lambda e: e.reduce_max(pmax[:, :], prb[:, 0:7], axis=AX),
                         reads=["prb"], writes=["pmax"])
                    S.op("vector", lambda e: e.tensor_scalar(
                        pm7[:, 0:7], ones7[:, 0:7], pmax[:, 0:1], None, ALU.mult),
                        reads=["ones7", "pmax"], writes=["pm7"])
                    S.op("vector", lambda e: e.tensor_tensor(
                        equ[:, 0:7], prb[:, 0:7], pm7[:, 0:7], ALU.is_equal),
                        reads=["prb", "pm7"], writes=["equ"])
                    S.op("vector", lambda e: e.select(
                        eff[:, 0:7], equ[:, 0:7], ones7[:, 0:7], zero7[:, 0:7]),
                        reads=["equ", "ones7", "zero7"], writes=["eff"])
                    S.op("vector", lambda e: e.tensor_tensor_scan(
                        out=pre[:, 0:7], data0=ones7[:, 0:7], data1=eff[:, 0:7],
                        initial=0.0, op0=ALU.mult, op1=ALU.add),
                        reads=["ones7", "eff"], writes=["pre"])
                    S.op("vector", lambda e: e.tensor_tensor(
                        msk[:, 0:7], pre[:, 0:7], eff[:, 0:7], ALU.subtract),
                        reads=["pre", "eff"], writes=["msk"])
                    S.op("vector", lambda e: e.tensor_tensor(
                        msk[:, 0:7], eff[:, 0:7], msk[:, 0:7], ALU.subtract),
                        reads=["eff", "msk"], writes=["msk"])
                    S.op("vector", lambda e: e.tensor_scalar(
                        msk[:, 0:7], msk[:, 0:7], 0.0, None, ALU.max),
                        reads=["msk"], writes=["msk"])
                    S.op("vector", lambda e: e.tensor_scalar(
                        pclp[:, 0:1], pmax[:, 0:1], 1e-8, 1.0 - 1e-8, ALU.max, ALU.min),
                        reads=["pmax"], writes=["pclp"])
                    S.op("vector", lambda e: e.tensor_scalar(
                        pclp[:, 1:2], pclp[:, 0:1], 0.5, 1e-8, ALU.add, ALU.add),
                        reads=["pclp"], writes=["pclp"])
                    S.op("vector", lambda e: e.tensor_scalar(
                        pclp[:, 1:2], pclp[:, 1:2], 0.5, 2.0, ALU.max, ALU.min),
                        reads=["pclp"], writes=["pclp"])
                    S.op("vector", lambda e: e.reciprocal(pclp[:, 2:3], pclp[:, 1:2]),
                         reads=["pclp"], writes=["pclp"])
                    S.op("vector", lambda e, tt=tt: e.tensor_scalar(
                        w8cat[:, tt * 8:tt * 8 + 7], msk[:, 0:7], pclp[:, 0:1],
                        pclp[:, 2:3], ALU.mult, ALU.mult),
                        reads=["msk", "pclp"], writes=["w8c"])
                    S.op("vector", lambda e, tt=tt: e.tensor_scalar(
                        w8cat[:, tt * 8 + 7:tt * 8 + 8], pclp[:, 2:3], 0.5, None,
                        ALU.mult),
                        reads=["pclp"], writes=["w8c"])

            S.op("vector", lambda e: e.tensor_copy(w8b[:, :], w8cat[:, :]),
                 reads=["w8c"], writes=["w8b"])
            S.op("tensor", lambda e: e.transpose(pt[:, 0:128], w8b[:, :], id_s[:, :]),
                 reads=["w8b", "id"], writes=["pt"])
            S.op("vector", lambda e: e.tensor_copy(w8b[:, :], pt[:, 0:128]),
                 reads=["pt"], writes=["w8b"])
            S.dma("gpsimd", lambda e: e.dma_start(
                out=w8r_d.rearrange("(e8 tt) t -> tt e8 t", tt=16),
                in_=w8b[:, :]),
                reads=["w8b"], writes=["w8d"])

        if "p5" in stages:
            for tsp in range(2):
                tok0 = tsp * 1024
                for e_i in range(E + 1):
                    S.dma("gpsimd", lambda e, e_i=e_i, tsp=tsp: e.indirect_dma_start(
                        out=ab("wbc", 0, 1024), out_offset=None,
                        in_=w8r_d[:, :],
                        in_offset=IndirectOffsetOnAxis(
                            gidx_s[:, 84 + e_i * 2 + tsp:85 + e_i * 2 + tsp], 0)),
                        reads=["w8d", "gidx"], writes=bkeys("wbc", 0, 1024))

                    for mb in range(11):
                        par = mb % 2
                        wstk = fkeys("wstA", 0, 4096)
                        w13k = fkeys("xtA", par * 2048, 2048)
                        w13base = par * 4096
                        for hh in range(2):
                            S.dma("sync", lambda e, e_i=e_i, mb=mb, hh=hh: e.dma_start(
                                out=af("wstA", hh * 2048, 2048)
                                    .rearrange("p (c m) -> p c m", m=512),
                                in_=W13R[e_i, mb, hh * 512:(hh + 1) * 512, :]
                                    .rearrange("(c p) m -> p c m", p=128)),
                                writes=wstk)
                        S.op("scalar", lambda e, w13base=w13base: e.copy(
                            afb(w13base, 4096), af("wstA", 0, 4096)),
                            reads=wstk, writes=w13k)

                        for mi in range(2):
                            m = mb * 2 + mi
                            def mg(e, w13base=w13base, mi=mi, tok0=tok0, woff=0, p0=0):
                                last = None
                                for k in range(ND):
                                    lhs = afb(w13base + k * 512 + woff + mi * 128,
                                              128)
                                    for t2 in range(2):
                                        rhs = ARB[:, AB["hT"] + k * T + tok0 + t2 * TS:
                                                  AB["hT"] + k * T + tok0 + (t2 + 1) * TS]
                                        last = e.matmul(pbank[p0 + t2][:, :], lhs, rhs,
                                                        start=(k == 0), stop=(k == ND - 1))
                                return last
                            S.op("tensor", lambda e, mi=mi, tok0=tok0, w13base=w13base:
                                 mg(e, w13base, mi, tok0, 0, 0),
                                 reads=w13k + bkeys("hT", 0, ND * T),
                                 writes=["pb0", "pb1"])
                            for t2 in range(2):
                                S.op("scalar", lambda e, m=m, t2=t2: e.activation(
                                    ab("gg", m * 1024 + t2 * TS, TS),
                                    pbank[t2][:, :], AF.Silu),
                                    reads=[PB[t2]],
                                    writes=bkeys("gg", m * 1024, 1024))
                            g3p = 2 if mi % 2 == 0 else 4
                            S.op("tensor", lambda e, mi=mi, tok0=tok0, w13base=w13base,
                                 g3p=g3p: mg(e, w13base, mi, tok0, 256, g3p),
                                 reads=w13k + bkeys("hT", 0, ND * T),
                                 writes=[PB[g3p], PB[g3p + 1]])
                            so_ = 8192 + (mi % 2) * 1024
                            sck = fkeys("xtA", 4096 + (mi % 2) * 512, 512)
                            for t2 in range(2):
                                S.op("vector", lambda e, t2=t2, so_=so_, g3p=g3p:
                                     e.scalar_tensor_tensor(
                                         afb(so_ + t2 * TS, TS),
                                         pbank[g3p + t2][:, :], 1.0,
                                         ab("wbc", t2 * TS, TS),
                                         ALU.mult, ALU.mult),
                                     reads=[PB[g3p + t2]] + bkeys("wbc", 0, 1024),
                                     writes=sck)
                            for t2 in range(2):
                                S.op("gpsimd", lambda e, m=m, t2=t2, so_=so_:
                                     e.tensor_mul(
                                         ab("gg", m * 1024 + t2 * TS, TS),
                                         ab("gg", m * 1024 + t2 * TS, TS),
                                         afb(so_ + t2 * TS, TS)),
                                     reads=sck + bkeys("gg", m * 1024, 1024),
                                     writes=bkeys("gg", m * 1024, 1024))

                    for kh in range(2):
                        kn = 11
                        for kk in range(kn):
                            k = kh * 11 + kk
                            par = "AB"[kk % 2]
                            S.dma("sync", lambda e, e_i=e_i, k=k, par=par: e.dma_start(
                                out=af("wst" + par, 0, D),
                                in_=W2T[e_i, k * 128:(k + 1) * 128, :]),
                                writes=fkeys("wst" + par, 0, D))
                            S.op("vector", lambda e, k=k, kk=kk, par=par: e.tensor_copy(
                                ab("w2h", kk * 1024, 1024), af("wst" + par, 0, D)),
                                reads=fkeys("wst" + par, 0, D),
                                writes=bkeys("w2h", kk * 1024, 1024))
                        for tt2 in range(8):
                            def mw2(e, kh=kh, tt2=tt2, tok0=tok0):
                                last = None
                                for kk in range(11):
                                    k = kh * 11 + kk
                                    lhs = ARB[:, AB["gg"] + k * 1024 + tt2 * 128:
                                              AB["gg"] + k * 1024 + (tt2 + 1) * 128]
                                    for n in range(2):
                                        last = e.matmul(
                                            pbank[4 + n][:, :],
                                            lhs,
                                            ARB[:, AB["w2h"] + kk * 1024 + n * 512:
                                                AB["w2h"] + kk * 1024 + (n + 1) * 512],
                                            start=(kk == 0), stop=(kk == 10))
                                return last
                            S.op("tensor", mw2,
                                 reads=bkeys("gg", 0, NII * 1024) + bkeys("w2h", 0, 11 * 1024),
                                 writes=["pb4", "pb5"])
                            tt = tsp * 8 + tt2
                            for n in range(2):
                                if e_i == 0 and kh == 0:
                                    S.op("vector", lambda e, tt=tt, n=n: e.tensor_copy(
                                        ab("acc", tt * 1024 + n * 512, 512),
                                        pbank[4 + n][:, :]),
                                        reads=[PB[4 + n]],
                                        writes=bkeys("acc", tt * 1024, 1024))
                                else:
                                    S.op("vector", lambda e, tt=tt, n=n: e.tensor_add(
                                        ab("acc", tt * 1024 + n * 512, 512),
                                        ab("acc", tt * 1024 + n * 512, 512),
                                        pbank[4 + n][:, :]),
                                        reads=[PB[4 + n]] + bkeys("acc", tt * 1024, 1024),
                                        writes=bkeys("acc", tt * 1024, 1024))

            for tt in range(NTT):
                parx = "AB"[tt % 2]
                xa = af("x0" + parx, 0, D)
                xk = fkeys("x0" + parx, 0, D)
                S.dma("sync", lambda e, tt=tt, xa=xa: e.dma_start(
                    out=xa, in_=x1_d[tt * 128:(tt + 1) * 128, :]),
                    reads=["x1d%d" % tt], writes=xk)
                S.op("vector", lambda e, tt=tt, xa=xa: e.tensor_add(
                    af("zt", 0, D), ab("acc", tt * 1024, 1024), xa),
                    reads=bkeys("acc", tt * 1024, 1024) + xk,
                    writes=fkeys("zt", 0, D))
                S.dma("gpsimd", lambda e, tt=tt: e.dma_start(
                    out=OUT[tt * 128:(tt + 1) * 128, :], in_=af("zt", 0, D)),
                    reads=fkeys("zt", 0, D), writes=["outd"])

        if "res" in debug:
            DR = dbg("res", (DI, T), BF16)
            S.dma("gpsimd", lambda e: e.dma_start(out=DR, in_=res_d[:, :]),
                  reads=["resd%d" % i for i in range(NI)], writes=["dbgr"])
        if "cfwd" in debug:
            DF = dbg("cfwd", (8 * FWD_ROWS, T), BF16)
            S.dma("gpsimd", lambda e: e.dma_start(out=DF, in_=contrib_fwd[:, :]),
                  reads=(["cfwd_u%d_%d" % (f, h) for f in range(NI) for h in (0, 4)]
                         + ["cfwd_dt%d_%d" % (f, h) for f in range(NI) for h in (0, 4)]
                         + ["cfwd_bc%d" % gd for gd in range(8)]), writes=["dbgf"])
        if "a2af" in debug:
            DA2 = dbg("a2af", (8 * FWD_ROWS, T), BF16)
            S.dma("gpsimd", lambda e: e.dma_start(out=DA2, in_=a2a_fwd[:, :]),
                  reads=["a2af"], writes=["dbga"])
        if "ydbg" in debug:
            DY = dbg("ydbg", (8 * CH, T), BF16)
            S.dma("gpsimd", lambda e: e.dma_start(out=DY, in_=contrib_y[:, :]),
                  reads=["cy_%d_%d_%d" % (w, ib, h) for w in range(NW)
                         for ib in range(4) for h in (0, 4)], writes=["dbgy"])
        if "a2ay" in debug:
            DY2 = dbg("a2ay", (8 * CH, T), BF16)
            S.dma("gpsimd", lambda e: e.dma_start(out=DY2, in_=a2a_y[:, :]),
                  reads=["a2ay"], writes=["dbgy2"])
        if "x1" in debug:
            DX1 = dbg("x1", (T, D))
            S.dma("gpsimd", lambda e: e.dma_start(out=DX1, in_=x1_d[:, :]),
                  reads=["x1d%d" % tt for tt in range(NTT)], writes=["dbgx1"])

        final = dict(S.cnt)
        S.op("sync", lambda e: e.nop(),
             deps=[("ld", final.get("ld", 0))])
        S.op("gpsimd", lambda e: e.nop(),
             deps=[("gq", final.get("gq", 0)), ("aq", final.get("aq", 0))])

        @block.sync
        def _(sync):
            S.emit_engine("sync", sync, sems)

        @block.tensor
        def _(tensor):
            S.emit_engine("tensor", tensor, sems)

        @block.vector
        def _(vector):
            S.emit_engine("vector", vector, sems)

        @block.scalar
        def _(scalar):
            S.emit_engine("scalar", scalar, sems)

        @block.gpsimd
        def _(gpsimd):
            S.emit_engine("gpsimd", gpsimd, sems)

    return nc, dbg_outs


def kernel(**inputs):
    shared, per_core = _host_prep(inputs)
    nc, _ = build_program()
    in_maps = []
    for c in range(8):
        m = dict(shared)
        m.update(per_core[c])
        in_maps.append(m)
    res = run_bass_kernel_spmd(nc, in_maps, core_ids=list(range(8)))
    out = np.empty((B, L, D), np.float32)
    for c in range(8):
        b, g = c // 4, c % 4
        out[b, g * T:(g + 1) * T] = np.asarray(res.results[c]["out"], np.float32)
    return out
